# version 24
# baseline (speedup 1.0000x reference)
import numpy as np
import ml_dtypes
import concourse.bass as bass
import concourse.mybir as mybir
from concourse.bass_utils import run_bass_kernel_spmd
from concourse.expressions import RuntimeValue
from contextlib import ExitStack

F32 = mybir.dt.float32
BF16 = mybir.dt.bfloat16
U8 = mybir.dt.uint8
AF = mybir.ActivationFunctionType
OP = mybir.AluOpType
AX = mybir.AxisListType
ds = bass.ds
BF = ml_dtypes.bfloat16

TOK, DM = 2048, 2048
MYTOK, MYCH = 1024, 1024
NST = 16
TS, NCHUNK, NG = 512, 4, 8
HC, NHC = 1024, 16
NTILE = 16
EPS = 1e-5

ENG_SEMS = ("v", "a", "t", "gc")
DMA_SEMS = ("const", "smallw", "xasm", "xt0", "xt1", "gcol", "geB", "bmcd",
            "bc", "y", "w1a", "w1b", "w2", "acc", "acc2", "ccv", "ccl", "cca0", "cca1",
            "tap", "si", "xnw", "xnr", "ccb")
ENGS = ("sy", "v", "a", "t", "g")


def build_nc(taps=False):
    nc = bass.Bass("TRN2", target_bir_lowering=False)

    xb = nc.dram_tensor("xb", [TOK, DM], F32, kind="ExternalInput").ap()
    xbp = nc.dram_tensor("xbp", [TOK, DM], F32, kind="ExternalInput").ap()
    wxt = nc.dram_tensor("wxt", [DM, 96], F32, kind="ExternalInput").ap()
    wdtb = nc.dram_tensor("wdtb", [65, MYCH], F32, kind="ExternalInput").ap()
    alog = nc.dram_tensor("alog", [128, NG * NST], F32, kind="ExternalInput").ap()
    dre = nc.dram_tensor("dre", [128, NG], F32, kind="ExternalInput").ap()
    wrt = nc.dram_tensor("wrt", [2, DM], F32, kind="ExternalInput").ap()
    w1t = nc.dram_tensor("w1t", [2, DM, 8192], F32, kind="ExternalInput").ap()
    w2t = nc.dram_tensor("w2t", [2, 8192, DM], F32, kind="ExternalInput").ap()
    identd = nc.dram_tensor("identd", [128, 128], BF16, kind="ExternalInput").ap()
    out = nc.dram_tensor("out", [MYTOK, DM], F32, kind="ExternalOutput").ap()
    if taps:
        o_xn = nc.dram_tensor("o_xn", [128, 16 * TOK], BF16, kind="ExternalOutput").ap()
        o_cols = nc.dram_tensor("o_cols", [128, 256], F32, kind="ExternalOutput").ap()
        o_carry = nc.dram_tensor("o_carry", [128, 128], F32, kind="ExternalOutput").ap()
        o_dtx = nc.dram_tensor("o_dtx", [65, TOK], BF16, kind="ExternalOutput").ap()
        o_ydr = nc.dram_tensor("o_ydr", [TOK, MYCH], BF16, kind="ExternalOutput").ap()
        o_geB = nc.dram_tensor("o_geB", [128, 2 * MYTOK], BF16, kind="ExternalOutput").ap()

    ydr_t = nc.dram_tensor("ydr", [TOK, MYCH], BF16)
    ccd_t = nc.dram_tensor("ccd", [2 * TOK, MYCH], BF16)
    ydr = ydr_t.ap()
    ccd = ccd_t.ap()
    gdr2 = nc.dram_tensor("gdr2", [128, 32], F32).ap()
    xb_flat = xb.bitcast(U8).rearrange("a b -> (a b)")
    bmcd = nc.dram_tensor("bmcd", [32, TOK], BF16).ap()
    xnd = nc.dram_tensor("xnd", [128, 16 * TOK * 2], U8).ap()

    SEQ = {k: [] for k in ENGS}
    CNT = {k: 0 for k in ENG_SEMS + DMA_SEMS}

    def emit(eng, fn):
        SEQ[eng].append(fn)

    def res(key, n):
        CNT[key] += n
        return CNT[key]

    RT = {}

    def get_base(e, key, mult):
        k2 = (key, mult)
        if k2 not in RT:
            pid = nc.partition_id([e.engine])
            r = e.alloc_register(f"hb_{key}_{mult}")
            e.reg_alu(r, pid, 2, OP.mod)
            e.reg_alu(r, r, mult, OP.mult)
            RT[k2] = r
        return RT[k2]

    def off_reg(e, key, idx=0):
        k2 = ("off", key, idx)
        if k2 not in RT:
            RT[k2] = e.alloc_register(f"off_{key}_{idx}")
        return RT[k2]

    def persist_off(e, key, mult, const):
        k2 = ("p", key, mult, const)
        if k2 not in RT:
            rb = get_base(e, key, mult)
            r = e.alloc_register(f"po_{key}_{const}")
            e.reg_alu(r, rb, const, OP.add)
            RT[k2] = RuntimeValue(r, min_val=const, max_val=mult + const)
        return RT[k2]

    def rv(reg, lo, hi):
        return RuntimeValue(reg, min_val=lo, max_val=hi)

    with ExitStack() as ctx:
        def sb(nm, shp, dt):
            return ctx.enter_context(nc.sbuf_tensor(nm, shp, dt))

        def psum(nm, shp, dt):
            return ctx.enter_context(nc.psum_tensor(nm, shp, dt))

        xnfm = sb("xnfm", [128, 16 * TOK], BF16)
        xnu8 = xnfm[:, :].bitcast(U8)
        dtx = sb("dtx", [65, TOK], BF16)
        wdtbs = sb("wdtbs", [65, MYCH], BF16)
        Asb = sb("Asb", [128, NG * NST], F32)
        Dsb = sb("Dsb", [128, NG], F32)
        idn = sb("idn", [128, 128], BF16)
        carry = sb("carry", [128, 128], F32)
        gtok = sb("gtok", [128, 16], F32)
        stage2 = sb("stage2", [128, DM], BF16)
        cols = sb("cols", [128, 16 * 16], F32)

        (C_S1, C_S2, C_SXW0, C_SXW1, C_M, C_VPS, C_RSTD, C_NMR, C_D, C_SIG,
         C_GE, C_G0, C_G1, C_T0, C_T1, C_T2) = range(16)

        def col(idx, i):
            return cols[:, idx * 16 + i: idx * 16 + i + 1]

        def colsl(idx):
            return cols[:, idx * 16: idx * 16 + 16]

        UB = 134144
        union = sb("union", [128, UB], U8)

        def carve(off_bytes, nelem, dt):
            esz = {F32: 4, BF16: 2}[dt]
            return union[:, off_bytes: off_bytes + nelem * esz].bitcast(dt)

        class Cursor:
            def __init__(self):
                self.o = 0

            def take(self, nelem, dt):
                esz = {F32: 4, BF16: 2}[dt]
                ap = carve(self.o, nelem, dt)
                self.o += nelem * esz
                self.o = (self.o + 15) & ~15
                return ap

        cu = Cursor()
        w1pp = [cu.take(16 * 128, BF16), cu.take(16 * 128, BF16)]
        w2c = cu.take(8 * DM, BF16)
        aslot = cu.take(8 * MYTOK, BF16)
        stg = [cu.take(DM, BF16), None]
        fbuf = [cu.take(512, BF16) for _ in range(3)]
        sqb = [cu.take(512, BF16)]
        BmBb = cu.take(NST * TS, BF16)
        CmBb = cu.take(NST * TS, BF16)
        dtg = [cu.take(TS, F32), cu.take(TS, F32)]
        ug = [cu.take(TS, BF16), cu.take(TS, BF16)]
        dab = [cu.take(TS, BF16) for _ in range(4)]
        dbxb = [cu.take(TS, BF16) for _ in range(4)]
        hb = [cu.take(TS, BF16) for _ in range(4)]
        hC = [cu.take(TS, BF16) for _ in range(4)]
        ypb = cu.take(TS, BF16)
        ybuf2 = [cu.take(TS, BF16), cu.take(TS, BF16)]
        yst = [cu.take(MYCH, BF16) for _ in range(4)]
        t1s = cu.take(TS, F32)
        assert cu.o <= UB, cu.o

        cu = Cursor()
        xt = [cu.take(DM, F32), cu.take(DM, F32)]
        wrB = [cu.take(DM, F32), cu.take(DM, F32)]
        sqscr = cu.take(DM, BF16)
        sqg = cu.take(DM, BF16)
        bmcm = cu.take(TOK, F32)
        xnt = [cu.take(DM, BF16), cu.take(DM, BF16)]
        xasm = cu.take(DM, F32)
        wxts = cu.take(16 * 96, BF16)
        swc = cu.take(2, F32)
        assert cu.o <= UB, cu.o

        ccbig = carve(72 * 1024, 8 * MYCH, BF16)

        stg[1] = stage2[:, :]
        pm1 = [psum("pm1a", [128, 512], F32), psum("pm1b", [128, 512], F32),
               psum("pm1c", [128, 512], F32)]
        pm2 = [psum("pm2a", [128, 512], F32), psum("pm2b", [128, 512], F32)]
        pdt = psum("pdt", [128, 512], F32)
        ptr = [psum("ptra", [128, 512], BF16), psum("ptrb", [128, 512], BF16)]

        SM = {k: ctx.enter_context(nc.semaphore(f"s_{k}"))
              for k in ENG_SEMS + DMA_SEMS}
        block = ctx.enter_context(nc.Block())

        def W(e, key, val):
            if val:
                e.wait_ge(SM[key], val)

        class Ser:

            def __init__(self, key):
                self.key = key
                self.base = CNT[key]
                self.n = 0

            def op(self):
                w = self.base + self.n
                self.n += 1
                res(self.key, 1)
                return w

        def f(e):
            e.dma_start(out=idn[:, :], in_=identd).then_inc(SM["const"], 16)
            e.dma_start(out=Asb[:, :], in_=alog).then_inc(SM["const"], 16)
            e.dma_start(out=Dsb[:, :], in_=dre).then_inc(SM["const"], 16)
            e.dma_start(out=wrB[0][:, :].rearrange("p (o f) -> p o f", o=1),
                        in_=wrt[0:1, :].partition_broadcast(128)).then_inc(SM["const"], 16)
            e.dma_start(out=wrB[1][:, :].rearrange("p (o f) -> p o f", o=1),
                        in_=wrt[1:2, :].partition_broadcast(128)).then_inc(SM["const"], 16)
            e.sem_inc(SM["si"], 1)
        emit("sy", f)
        CONSTS = res("const", 5 * 16)
        SI_CONST = res("si", 1)

        def f(e):
            e.dma_start(out=wxts[:, :].rearrange("p (a m) -> p a m", a=16),
                        in_=wxt.rearrange("(a p) m -> p a m", p=128)).then_inc(SM["smallw"], 16)
            e.dma_start(out=wdtbs[:, :], in_=wdtb).then_inc(SM["smallw"], 16)
            e.sem_inc(SM["gc"], 1)
        emit("g", f)
        SMALLW = res("smallw", 2 * 16)
        GC_SMALLW = res("gc", 1)

        for tt in range(8):
            w0 = CNT["xasm"]

            def f(e, tt=tt, w0=w0):
                rbX = persist_off(e, "syx", 1024 * DM * 4, 0)
                base = tt * 128 * DM * 4
                win = xb_flat[base: base + 1024 * DM * 4 + 128 * DM * 4]
                W(e, "xasm", w0)
                e.dma_start(out=xasm[:, :],
                            in_=win[ds(rbX, 128 * DM * 4)]
                                .rearrange("(p f) -> p f", f=DM * 4).bitcast(F32)
                            ).then_inc(SM["xasm"], 16)
                W(e, "xasm", w0 + 16)
                e.dma_start(out=out[tt * 128:(tt + 1) * 128, :],
                            in_=xasm[:, :]).then_inc(SM["xasm"], 16)
            emit("sy", f)
            res("xasm", 32)
        XASM = CNT["xasm"]

        sa = Ser("a")
        wA0 = sa.op()

        def f(e, w=wA0):
            W(e, "si", SI_CONST)
            W(e, "const", CONSTS)
            W(e, "a", w)
            e.activation(Asb[:, :], Asb[:, :], AF.Exp).then_inc(SM["a"], 1)
        emit("a", f)
        a_Aexp = CNT["a"]

        sv = Ser("v")
        wV0 = sv.op()

        def f(e, w=wV0, wa=a_Aexp):
            W(e, "a", wa)
            W(e, "v", w)
            e.tensor_scalar(Asb[:, :], Asb[:, :], -1.0, None, OP.mult).then_inc(SM["v"], 1)
        emit("v", f)
        v_Aneg = CNT["v"]

        vWAR = [0] * NTILE
        aWAR = [0] * NTILE
        gWAR = [0] * NTILE
        t_tr = {}
        a_ev = {}
        xp_a = {}
        xp_v = {}
        t_xp = {}

        def emit_xp(tc):
            def f(e, tc=tc, wvv=xp_v.get(tc - 2, 0),
                  wa=max(a_ev[(4 * tc + 3) * 16 + 15], xp_a.get(tc - 2, 0))):
                W(e, "gc", GC_SMALLW)
                W(e, "smallw", SMALLW)
                W(e, "v", wvv)
                W(e, "a", wa)
                p = pm1[tc % 2]
                for dc in range(16):
                    mm = e.matmul(p[0:96, :], wxts[:, dc * 96:(dc + 1) * 96],
                                  xnfm[:, dc * TOK + tc * 512: dc * TOK + (tc + 1) * 512],
                                  start=(dc == 0), stop=(dc == 15))
                mm.then_inc(SM["t"], 1)
            emit("t", f)
            t_xp[tc] = res("t", 1)

            s = Ser("a")
            w1_ = s.op()

            def f(e, tc=tc, wt=t_xp[tc], w1_=w1_):
                W(e, "t", wt)
                W(e, "a", w1_)
                e.activation(dtx[0:64, tc * 512:(tc + 1) * 512],
                             pm1[tc % 2][0:64, :], AF.Identity).then_inc(SM["a"], 1)
            emit("a", f)
            xp_a[tc] = CNT["a"]

            s = Ser("v")
            w1_ = s.op()

            def f(e, tc=tc, wt=t_xp[tc], w1_=w1_):
                W(e, "t", wt)
                W(e, "v", w1_)
                e.tensor_copy(bmcm[64:96, tc * 512:(tc + 1) * 512],
                              pm1[tc % 2][64:96, :]).then_inc(SM["v"], 1)
            emit("v", f)
            xp_v[tc] = CNT["v"]

        for i in range(NTILE):
            xk = "xt%d" % (i % 2)

            def f(e, i=i, xk=xk, wv=(vWAR[i - 2] if i >= 2 else 0),
                  wa=(aWAR[i - 2] if i >= 2 else 0),
                  wg=0):
                W(e, "v", wv)
                W(e, "a", wa)
                e.dma_start(out=xt[i % 2][:, :],
                            in_=xbp[i * 128:(i + 1) * 128, :]).then_inc(SM[xk], 16)
                e.sem_inc(SM["si"], 1)
            emit("sy", f)
            xt_ms = res(xk, 16)
            si_xt = res("si", 1)

            s = Ser("v")
            ws = [s.op() for _ in range(9)]

            def f(e, i=i, xk=xk, wx=xt_ms, ws=ws, wsi=si_xt):
                W(e, "si", wsi)
                W(e, xk, wx)
                W(e, "const", CONSTS)
                x = xt[i % 2][:, :]
                W(e, "v", ws[0])
                e.tensor_reduce(col(C_S1, i), x, AX.X, OP.add).then_inc(SM["v"], 1)
                W(e, "v", ws[1])
                e.scalar_tensor_tensor(sqscr[:, :], x, 1.0, x, OP.mult, OP.mult,
                                       accum_out=col(C_S2, i)).then_inc(SM["v"], 1)
                W(e, "v", ws[2])
                e.scalar_tensor_tensor(sqscr[:, :], x, 1.0, wrB[0][:, :], OP.mult, OP.mult,
                                       accum_out=col(C_SXW0, i)).then_inc(SM["v"], 1)
                W(e, "v", ws[3])
                e.scalar_tensor_tensor(sqscr[:, :], x, 1.0, wrB[1][:, :], OP.mult, OP.mult,
                                       accum_out=col(C_SXW1, i)).then_inc(SM["v"], 1)
                W(e, "v", ws[4])
                e.tensor_scalar(col(C_M, i), col(C_S1, i), 1.0 / DM, None, OP.mult).then_inc(SM["v"], 1)
                W(e, "v", ws[5])
                e.tensor_scalar(col(C_T0, i), col(C_S2, i), 1.0 / DM, None, OP.mult).then_inc(SM["v"], 1)
                W(e, "v", ws[6])
                e.tensor_tensor(col(C_T1, i), col(C_M, i), col(C_M, i), OP.mult).then_inc(SM["v"], 1)
                W(e, "v", ws[7])
                e.tensor_tensor(col(C_T2, i), col(C_T0, i), col(C_T1, i), OP.subtract).then_inc(SM["v"], 1)
                W(e, "v", ws[8])
                e.tensor_scalar(col(C_VPS, i), col(C_T2, i), EPS, None, OP.add).then_inc(SM["v"], 1)
            emit("v", f)
            v_stats = CNT["v"]

            s = Ser("a")
            ws = [s.op() for _ in range(2)]

            def f(e, i=i, wv=v_stats, ws=ws):
                W(e, "v", wv)
                W(e, "a", ws[0])
                e.activation(col(C_T0, i), col(C_VPS, i), AF.Ln).then_inc(SM["a"], 1)
                W(e, "a", ws[1])
                e.activation(col(C_RSTD, i), col(C_T0, i), AF.Exp, scale=-0.5).then_inc(SM["a"], 1)
            emit("a", f)
            a_rstd = CNT["a"]

            s = Ser("v")
            ws = [s.op() for _ in range(2)]

            def f(e, i=i, wa=a_rstd, ws=ws):
                W(e, "a", wa)
                W(e, "v", ws[0])
                e.tensor_tensor(col(C_T1, i), col(C_M, i), col(C_RSTD, i), OP.mult).then_inc(SM["v"], 1)
                W(e, "v", ws[1])
                e.tensor_scalar(col(C_NMR, i), col(C_T1, i), -1.0, None, OP.mult).then_inc(SM["v"], 1)
            emit("v", f)
            vWAR[i] = CNT["v"]

            s = Ser("a")
            w1_ = s.op()

            def f(e, i=i, wv=vWAR[i], w1_=w1_):
                W(e, "v", wv)
                W(e, "a", w1_)
                e.activation(xnt[i % 2][:, :], xt[i % 2][:, :], AF.Identity,
                             bias=col(C_NMR, i), scale=col(C_RSTD, i)).then_inc(SM["a"], 1)
            emit("a", f)
            aWAR[i] = CNT["a"]

            for j in range(16):
                li = i * 16 + j

                def f(e, i=i, j=j, li=li, wa=max(aWAR[i], a_ev.get(li - 2, 0))):
                    if li == 0:
                        W(e, "const", CONSTS)
                    W(e, "a", wa)
                    e.transpose(ptr[j % 2][:, 0:128],
                                xnt[i % 2][:, j * 128:(j + 1) * 128],
                                idn[:, :]).then_inc(SM["t"], 1)
                emit("t", f)
                t_tr[li] = res("t", 1)

                s = Ser("a")
                w1_ = s.op()

                def f(e, i=i, j=j, wt=t_tr[li], w1_=w1_):
                    W(e, "t", wt)
                    W(e, "a", w1_)
                    e.activation(xnfm[:, j * TOK + i * 128: j * TOK + (i + 1) * 128],
                                 ptr[j % 2][:, 0:128], AF.Identity).then_inc(SM["a"], 1)
                emit("a", f)
                a_ev[li] = CNT["a"]

            if i % 4 == 3:
                emit_xp(i // 4)

        xn_done_a = CNT["a"]

        s = Ser("v")
        ws = [s.op() for _ in range(10)]

        def f(e, ws=ws):
            ops = [
                lambda: e.tensor_reduce(swc[:, 0:1], wrB[0][:, :], AX.X, OP.add),
                lambda: e.tensor_reduce(swc[:, 1:2], wrB[1][:, :], AX.X, OP.add),
                lambda: e.tensor_tensor(colsl(C_T0), colsl(C_M),
                                        swc[:, 0:1].broadcast_to([128, 16]), OP.mult),
                lambda: e.tensor_tensor(colsl(C_T1), colsl(C_SXW0), colsl(C_T0), OP.subtract),
                lambda: e.tensor_tensor(colsl(C_G0), colsl(C_T1), colsl(C_RSTD), OP.mult),
                lambda: e.tensor_tensor(colsl(C_T0), colsl(C_M),
                                        swc[:, 1:2].broadcast_to([128, 16]), OP.mult),
                lambda: e.tensor_tensor(colsl(C_T1), colsl(C_SXW1), colsl(C_T0), OP.subtract),
                lambda: e.tensor_tensor(colsl(C_G1), colsl(C_T1), colsl(C_RSTD), OP.mult),
                lambda: e.tensor_tensor(colsl(C_D), colsl(C_G0), colsl(C_G1), OP.subtract),
            ]
            for w, op_ in zip(ws, ops):
                W(e, "v", w)
                op_().then_inc(SM["v"], 1)
        emit("v", f)
        v_dcol = CNT["v"] - 1
        res("v", 0)
        CNT["v"] -= 1
        v_dcol = CNT["v"]

        s = Ser("a")
        w1_ = s.op()

        def f(e, wv=v_dcol, w1_=w1_):
            W(e, "v", wv)
            W(e, "a", w1_)
            e.activation(colsl(C_SIG), colsl(C_D), AF.Sigmoid).then_inc(SM["a"], 1)
        emit("a", f)
        a_sig = CNT["a"]

        s = Ser("v")
        ws = [s.op() for _ in range(5)]

        def f(e, wa=a_sig, ws=ws):
            W(e, "a", wa)
            ops = [
                lambda: e.tensor_scalar(colsl(C_GE), colsl(C_D), 0.0, None, OP.is_ge),
                lambda: e.tensor_tensor(colsl(C_G0), colsl(C_SIG), colsl(C_GE), OP.mult),
                lambda: e.tensor_scalar(colsl(C_T0), colsl(C_SIG), -1.0, 1.0, OP.mult, OP.add),
                lambda: e.tensor_scalar(colsl(C_T1), colsl(C_GE), -1.0, 1.0, OP.mult, OP.add),
                lambda: e.tensor_tensor(colsl(C_G1), colsl(C_T0), colsl(C_T1), OP.mult),
            ]
            for w, op_ in zip(ws, ops):
                W(e, "v", w)
                op_().then_inc(SM["v"], 1)
        emit("v", f)
        v_g = CNT["v"]

        def f(e, w=v_g):
            W(e, "v", w)
            for hh in range(2):
                e.dma_start(out=gdr2[:, hh * 16: hh * 16 + 8],
                            in_=cols[:, C_G0 * 16 + hh * 8: C_G0 * 16 + hh * 8 + 8]
                            ).then_inc(SM["gcol"], 16)
                e.dma_start(out=gdr2[:, hh * 16 + 8: hh * 16 + 16],
                            in_=cols[:, C_G1 * 16 + hh * 8: C_G1 * 16 + hh * 8 + 8]
                            ).then_inc(SM["gcol"], 16)
            e.sem_inc(SM["si"], 1)
        emit("sy", f)
        GCOL = res("gcol", 64)
        SI_GCOL = res("si", 1)

        def f(e):
            rbx = persist_off(e, "ggx", 16, 0)
            W(e, "si", SI_GCOL)
            W(e, "gcol", GCOL)
            e.dma_start(out=gtok[:, 0:16],
                        in_=gdr2[:, ds(rbx, 16)]).then_inc(SM["geB"], 16)
            e.sem_inc(SM["si"], 1)
        emit("sy", f)
        GEB = res("geB", 16)
        SI_GEB = res("si", 1)

        s = Ser("v")
        w1_ = s.op()

        def f(e, w=xp_v[3], w1_=w1_):
            W(e, "v", max(w, w1_))
            e.memset(dtx[64:65, :], 1.0).then_inc(SM["v"], 1)
        emit("v", f)
        v_ones = CNT["v"]

        def f(e, w=v_ones):
            W(e, "v", w)
            e.dma_start(out=bmcd, in_=bmcm[64:96, :]).then_inc(SM["bmcd"], 16)
            e.sem_inc(SM["gc"], 1)
        emit("g", f)
        BMCD = res("bmcd", 16)
        GC_BMCD = res("gc", 1)

        def f(e, wa=xn_done_a):
            W(e, "a", wa)
            e.dma_start(out=xnd, in_=xnu8).then_inc(SM["xnw"], 16)
            e.sem_inc(SM["si"], 1)
        emit("sy", f)
        XNW = res("xnw", 16)
        SI_XNW = res("si", 1)

        def f(e, wt=t_xp[3]):
            rbB = persist_off(e, "sy", 2048, 0)
            W(e, "si", SI_XNW)
            W(e, "xnw", XNW)
            W(e, "t", wt)
            e.dma_start(
                out=xnu8[:, 8 * 4096: 16 * 4096].rearrange("p (a f) -> p a f", a=16),
                in_=xnd.rearrange("p (a f) -> p a f", a=16)[:, :, ds(rbB, 2048)],
            ).then_inc(SM["xnr"], 16)
            e.sem_inc(SM["si"], 1)
        emit("sy", f)
        XNR = res("xnr", 16)
        SI_XNR = res("si", 1)

        EARLY = dict(v=v_ones, a=xp_a[3], t=t_xp[3], gc=GC_BMCD,
                     si=max(SI_XNR, SI_GEB))
        EARLY_DMA = dict(const=CONSTS, smallw=SMALLW, xasm=XASM, bmcd=BMCD,
                         gcol=GCOL, geB=GEB, xnr=XNR)

        for eng in ENGS:
            def f(e, eng=eng):
                for k2, val in EARLY.items():
                    W(e, k2, val)
                for k2, val in EARLY_DMA.items():
                    W(e, k2, val)
            emit(eng, f)

        ffn_acc = {}
        a_ev2 = {}
        v_stt = {}
        v_am = {}
        a_sq = {}
        a_ev1 = {}
        t_m1 = {}
        t_m2 = {}
        t_m2l = {}
        v_scan = {}
        g_hC = {}
        g_dbx = {}
        a_dA = {}
        v_dtg = {}
        a_sp = {}
        g_tree = {}
        g_ypb = {}
        v_ybuf = {}
        t_ytr = {}
        a_yev = {}
        y_ms = {}
        bc_ms = {}
        w1_ms = {}
        w2_ms = {}
        w1_gc = {}
        w2_gc = {}
        acc_gc = {}
        si_bc = {}
        si_y = {}

        NSTRIPS = NHC * 8
        YPC = 128 if taps else 64

        def w1_load(strip):
            wk = "w1%s" % ("ab"[strip % 2])

            def f(e, strip=strip, wk=wk, wt=t_m1.get(strip - 2, 0)):
                W(e, "t", wt)
                hc, sидx = strip // 8, strip % 8
                e_idx = hc // 8
                c0 = (hc % 8) * HC + sидx * 128
                e.dma_start(
                    out=w1pp[strip % 2][:, :].rearrange("p (a f) -> p a f", a=16),
                    in_=w1t[e_idx:e_idx + 1, :, c0:c0 + 128].rearrange(
                        "o (a p) f -> p (o a) f", p=128),
                ).then_inc(SM[wk], 16)
                e.sem_inc(SM["gc"], 1)
            emit("g", f)
            w1_ms[strip] = res(wk, 16)
            w1_gc[strip] = res("gc", 1)

        def w2_load(hc):
            def f(e, hc=hc, wt=t_m2.get(hc - 1, 0)):
                W(e, "t", wt)
                e_idx = hc // 8
                r0 = (hc % 8) * HC
                e.dma_start(
                    out=w2c[:, :].rearrange("p (a f) -> p a f", a=8),
                    in_=w2t[e_idx:e_idx + 1, r0:r0 + HC, :].rearrange(
                        "o (a p) f -> p (o a) f", p=128),
                ).then_inc(SM["w2"], 16)
                e.sem_inc(SM["gc"], 1)
            emit("g", f)
            w2_ms[hc] = res("w2", 16)
            w2_gc[hc] = res("gc", 1)

        w1_load(0)
        w1_load(1)
        w2_load(0)

        def emit_ffn_strips(hc, scan_sus=None):
            e_idx = hc // 8
            for sx in range(8):
                strip = hc * 8 + sx
                wk = "w1%s" % ("ab"[strip % 2])

                t_m1h = {}
                for tt in range(2):
                    q = strip * 2 + tt

                    def f(e, strip=strip, tt=tt, q=q, wk=wk, wg=w1_ms[strip],
                          wgc=w1_gc[strip], wv=a_ev1.get(q - 3, 0)):
                        if tt == 0:
                            W(e, "gc", wgc)
                            W(e, wk, wg)
                        W(e, "a", wv)
                        for dc in range(16):
                            base = 8 * 4096 + dc * 2048 + tt * 1024
                            mm = e.matmul(pm1[q % 3][:, :],
                                          w1pp[strip % 2][:, dc * 128:(dc + 1) * 128],
                                          xnu8[:, base: base + 1024].bitcast(BF16),
                                          start=(dc == 0), stop=(dc == 15))
                        mm.then_inc(SM["t"], 1)
                    emit("t", f)
                    t_m1h[q] = res("t", 1)
                t_m1[strip] = t_m1h[strip * 2 + 1]
                if strip + 2 < NSTRIPS:
                    w1_load(strip + 2)

                if sx >= 1:
                    emit_act_chain(hc, strip - 1)
                if scan_sus is not None:
                    emit_scan_cluster_a(scan_sus[sx // 4], sx % 4)

                for tt in range(2):
                    q = strip * 2 + tt

                    s = Ser("a")
                    w1_ = s.op()

                    def f(e, strip=strip, tt=tt, q=q, wt=t_m1h[q], w1_=w1_):
                        W(e, "t", wt)
                        W(e, "a", w1_)
                        e.activation(fbuf[q % 3][:, :], pm1[q % 3][:, :],
                                     AF.Identity).then_inc(SM["a"], 1)
                    emit("a", f)
                    a_ev1[q] = CNT["a"]

                if sx == 7:
                    emit_act_chain(hc, strip)
                if scan_sus is not None:
                    emit_scan_cluster(scan_sus[sx // 4], sx % 4)

        def emit_act_chain(hc, strip):
            e_idx = hc // 8
            sx = strip % 8
            for tt in range(2):
                q = strip * 2 + tt

                s = Ser("v")
                w1_ = s.op()

                def f(e, q=q, wa=a_ev1[q], w1_=w1_, wsq=a_sq.get(q - 1, 0)):
                    W(e, "a", max(wa, wsq))
                    W(e, "v", w1_)
                    f0 = fbuf[q % 3][:, :]
                    e.scalar_tensor_tensor(sqb[0][:, :], f0, 0.5, f0,
                                           OP.mult, OP.max).then_inc(SM["v"], 1)
                emit("v", f)
                v_stt[q] = CNT["v"]

                s = Ser("a")
                w1_ = s.op()

                def f(e, sx=sx, tt=tt, q=q, wv=v_stt[q], w1_=w1_):
                    W(e, "v", wv)
                    W(e, "a", w1_)
                    e.activation(
                        aslot[:, sx * MYTOK + tt * 512: sx * MYTOK + (tt + 1) * 512],
                        sqb[0][:, :], AF.Square).then_inc(SM["a"], 1)
                emit("a", f)
                a_sq[q] = CNT["a"]

        def emit_ffn_m2(hc):
            a_ready = a_sq[(hc * 8 + 7) * 2 + 1]
            for tt in range(8):
                for dq in range(4):
                    m = (hc * 8 + tt) * 4 + dq

                    def f(e, hc=hc, tt=tt, dq=dq, m=m, wv=a_ready,
                          wg=w2_ms[hc], wgc=w2_gc[hc], wae=a_ev2.get(m - 2, 0)):
                        W(e, "a", wv)
                        W(e, "gc", wgc)
                        W(e, "w2", wg)
                        W(e, "a", wae)
                        p = pm2[m % 2]
                        for sx in range(8):
                            mm = e.matmul(p[:, :],
                                          aslot[:, sx * MYTOK + tt * 128: sx * MYTOK + tt * 128 + 128],
                                          w2c[:, sx * DM + dq * 512: sx * DM + (dq + 1) * 512],
                                          start=(sx == 0), stop=(sx == 7))
                        mm.then_inc(SM["t"], 1)
                    emit("t", f)
                    t_m2l[m] = res("t", 1)

                    s = Ser("a")
                    w1_ = s.op()

                    def f(e, hc=hc, tt=tt, dq=dq, m=m, wt=t_m2l[m], w1_=w1_,
                          wgacc=acc_gc.get(hc * 8 + tt - 2, 0)):
                        L = hc * 8 + tt
                        W(e, "t", wt)
                        if dq == 0 and wgacc:
                            W(e, "gc", wgacc)
                            W(e, "acc" if L % 2 == 0 else "acc2",
                              16 * (L // 2))
                        W(e, "a", w1_)
                        e.activation(stg[L % 2][:, dq * 512:(dq + 1) * 512],
                                     pm2[m % 2][:, :], AF.Identity,
                                     scale=gtok[:, (hc // 8) * 8 + tt:
                                                (hc // 8) * 8 + tt + 1]).then_inc(SM["a"], 1)
                    emit("a", f)
                    a_ev2[m] = CNT["a"]

                L = hc * 8 + tt

                def f(e, hc=hc, tt=tt, L=L, wa=a_ev2[(hc * 8 + tt) * 4 + 3]):
                    W(e, "a", wa)
                    W(e, "xasm", XASM)
                    e.dma_start(out=out[tt * 128:(tt + 1) * 128, :], in_=stg[L % 2][:, :],
                                accum_op=OP.add).then_inc(
                                    SM["acc" if L % 2 == 0 else "acc2"], 16)
                    e.sem_inc(SM["gc"], 1)
                emit("g", f)
                ffn_acc[L] = res("acc" if L % 2 == 0 else "acc2", 16)
                acc_gc[L] = res("gc", 1)

            t_m2[hc] = t_m2l[(hc * 8 + 7) * 4 + 3]
            if hc + 1 < NHC:
                w2_load(hc + 1)

        def emit_tree(su, qq):
            s = Ser("v")
            ws = [s.op() for _ in range(4)]

            def f(e, su=su, qq=qq, ws=ws):
                W(e, "v", ws[0])
                e.tensor_tensor(hC[0][:, :], hC[0][:, :], hC[1][:, :], OP.add).then_inc(SM["v"], 1)
                W(e, "v", ws[1])
                e.tensor_tensor(hC[2][:, :], hC[2][:, :], hC[3][:, :], OP.add).then_inc(SM["v"], 1)
                W(e, "v", ws[2])
                e.tensor_tensor(hC[0][:, :], hC[0][:, :], hC[2][:, :], OP.add).then_inc(SM["v"], 1)
                W(e, "v", ws[3])
                if qq == 0:
                    e.tensor_copy(ypb[:, :], hC[0][:, :]).then_inc(SM["v"], 1)
                else:
                    e.tensor_tensor(ypb[:, :], ypb[:, :], hC[0][:, :],
                                    OP.add).then_inc(SM["v"], 1)
            emit("v", f)

        def emit_scan_head(su):
            k, gi = su // 8, su % 8
            pp = su % 2

            if gi == 0:
                def f(e, k=k,
                      wv=v_ybuf.get(su - 1, 0)):
                    W(e, "bmcd", BMCD)
                    W(e, "v", wv)
                    e.dma_start(out=BmBb[:, :].rearrange("p (n t) -> p n t", n=NST),
                                in_=bmcd[0:16, k * TS:(k + 1) * TS].partition_broadcast(128)
                                ).then_inc(SM["bc"], 16)
                    e.dma_start(out=CmBb[:, :].rearrange("p (n t) -> p n t", n=NST),
                                in_=bmcd[16:32, k * TS:(k + 1) * TS].partition_broadcast(128)
                                ).then_inc(SM["bc"], 16)
                    e.sem_inc(SM["si"], 1)
                emit("sy", f)
                bc_ms[k] = res("bc", 32)
                si_bc[k] = res("si", 1)

            def f(e, su=su, k=k, gi=gi, wfree=v_dtg.get(su - 1, 0)):
                W(e, "v", wfree)
                e.matmul(pdt[:, :], wdtbs[:, gi * 128:(gi + 1) * 128],
                         dtx[:, k * TS:(k + 1) * TS], start=True, stop=True).then_inc(SM["t"], 1)
            emit("t", f)
            t_dt = res("t", 1)

            s = Ser("a")
            ws = [s.op() for _ in range(3)]

            def f(e, su=su, wt=t_dt, wfree=v_dtg.get(su - 1, 0), ws=ws):
                W(e, "t", wt)
                W(e, "v", wfree)
                W(e, "a", ws[0])
                e.activation(t1s[:, :], pdt[:, :], AF.Abs).then_inc(SM["a"], 1)
                W(e, "a", ws[1])
                e.activation(t1s[:, :], t1s[:, :], AF.Exp, scale=-1.0).then_inc(SM["a"], 1)
                W(e, "a", ws[2])
                e.activation(t1s[:, :], t1s[:, :], AF.Ln, bias=1.0).then_inc(SM["a"], 1)
            emit("a", f)
            a_sp[su] = CNT["a"]

            s = Ser("v")
            ws = [s.op() for _ in range(3)]

            def f(e, su=su, k=k, gi=gi, pp=pp, wt=t_dt, wa=a_sp[su],
                  wdA=a_dA.get((su - 2, NST - 1), 0), ws=ws):
                xnsl = xnu8[:, gi * 4096 + k * 1024: gi * 4096 + (k + 1) * 1024]
                W(e, "t", wt)
                W(e, "a", max(wa, wdA))
                W(e, "v", ws[0])
                e.tensor_scalar(dtg[pp][:, :], pdt[:, :], 0.0, None, OP.max).then_inc(SM["v"], 1)
                W(e, "v", ws[1])
                e.tensor_tensor(dtg[pp][:, :], dtg[pp][:, :], t1s[:, :], OP.add).then_inc(SM["v"], 1)
                W(e, "v", ws[2])
                e.tensor_tensor(ug[pp][:, :], dtg[pp][:, :],
                                xnsl.bitcast(BF16),
                                OP.mult).then_inc(SM["v"], 1)
            emit("v", f)
            v_dtg[su] = CNT["v"]

        def emit_scan_cluster_a(su, c):
            gi = su % 8
            for n in range(4 * c, 4 * c + 4):
                tile = gi * NST + n
                wfree_scan = (v_scan.get((su, n - 4), 0) if n >= 4
                              else v_scan.get((su - 1, NST - 4 + n), 0))

                s = Ser("a")
                w1_ = s.op()

                def f(e, su=su, n=n, tile=tile, wv=v_dtg[su],
                      wfree=wfree_scan, wA=v_Aneg, w1_=w1_):
                    W(e, "v", max(wv, wA, wfree))
                    W(e, "a", w1_)
                    e.activation(dab[n % 4][:, :], dtg[su % 2][:, :], AF.Exp,
                                 scale=Asb[:, tile:tile + 1]).then_inc(SM["a"], 1)
                emit("a", f)
                a_dA[(su, n)] = CNT["a"]

        def emit_scan_cluster(su, c):
            k, gi = su // 8, su % 8

            for n in range(4 * c, 4 * c + 4):
                tile = gi * NST + n
                npp = n % 4

                s = Ser("v")
                ws = [s.op() for _ in range(4)]

                def f(e, su=su, k=k, n=n, npp=npp, tile=tile,
                      wa=a_dA[(su, n)], ws=ws):
                    W(e, "si", si_bc[k])
                    W(e, "bc", bc_ms[k])
                    W(e, "a", wa)
                    init = 0.0 if k == 0 else carry[:, tile:tile + 1]
                    W(e, "v", ws[0])
                    e.tensor_tensor(dbxb[npp][:, :], ug[su % 2][:, :],
                                    BmBb[:, n * TS:(n + 1) * TS], OP.mult).then_inc(SM["v"], 1)
                    W(e, "v", ws[1])
                    e.tensor_tensor_scan(hb[npp][:, :], dab[n % 4][:, :], dbxb[npp][:, :],
                                         init, OP.mult, OP.add).then_inc(SM["v"], 1)
                    W(e, "v", ws[2])
                    e.tensor_copy(carry[:, tile:tile + 1], hb[npp][:, TS - 1:TS]).then_inc(SM["v"], 1)
                    W(e, "v", ws[3])
                    e.tensor_tensor(hC[n % 4][:, :], hb[npp][:, :],
                                    CmBb[:, n * TS:(n + 1) * TS], OP.mult).then_inc(SM["v"], 1)
                emit("v", f)
                v_scan[(su, n)] = CNT["v"]

            emit_tree(su, c)
            if c == 3:
                k, gi = su // 8, su % 8

                s = Ser("v")
                w1_ = s.op()

                def f(e, su=su, k=k, gi=gi, wt=t_ytr.get(su - 2, 0), w1_=w1_):
                    xnsl = xnu8[:, gi * 4096 + k * 1024: gi * 4096 + (k + 1) * 1024]
                    W(e, "t", wt)
                    W(e, "v", w1_)
                    e.scalar_tensor_tensor(ybuf2[su % 2][:, :],
                                           xnsl.bitcast(BF16),
                                           Dsb[:, gi:gi + 1], ypb[:, :],
                                           OP.mult, OP.add).then_inc(SM["v"], 1)
                emit("v", f)
                v_ybuf[su] = CNT["v"]

        def emit_ytr(su):
            k, gi = su // 8, su % 8
            ytr_t = {}
            ytr_a = {}
            for tsub in range(4):
                def f(e, su=su, tsub=tsub, wv=v_ybuf[su],
                      wa=(a_yev.get(su - 1, 0) if tsub < 2 else ytr_a[tsub - 2])):
                    W(e, "v", wv)
                    W(e, "a", wa)
                    e.transpose(ptr[tsub % 2][:, 0:128],
                                ybuf2[su % 2][:, tsub * 128:(tsub + 1) * 128],
                                idn[:, :]).then_inc(SM["t"], 1)
                emit("t", f)
                ytr_t[tsub] = res("t", 1)

                s = Ser("a")
                w1_ = s.op()

                def f(e, su=su, k=k, gi=gi, tsub=tsub, wt=ytr_t[tsub], w1_=w1_,
                      wsi=(si_y.get(k - 1, 0))):
                    W(e, "t", wt)
                    if wsi:
                        W(e, "si", wsi)
                    W(e, "y", YPC * k)
                    W(e, "a", w1_)
                    e.activation(yst[tsub][:, gi * 128:(gi + 1) * 128],
                                 ptr[tsub % 2][:, 0:128],
                                 AF.Identity).then_inc(SM["a"], 1)
                emit("a", f)
                ytr_a[tsub] = CNT["a"]
            t_ytr[su] = ytr_t[3]
            a_yev[su] = ytr_a[3]

            if gi == 7:
                def f(e, k=k, wa=a_yev[su]):
                    W(e, "a", wa)
                    for tsub in range(4):
                        e.dma_start(out=ydr[k * TS + tsub * 128: k * TS + (tsub + 1) * 128, :],
                                    in_=yst[tsub][:, :]).then_inc(SM["y"], 16)
                        if taps:
                            e.dma_start(out=o_ydr[k * TS + tsub * 128: k * TS + (tsub + 1) * 128, :],
                                        in_=yst[tsub][:, :]).then_inc(SM["y"], 16)
                    e.sem_inc(SM["si"], 1)
                emit("sy", f)
                y_ms[k] = res("y", YPC)
                si_y[k] = res("si", 1)

        ccv_ms = {}
        ccv_gc = {}

        def emit_cc(k):
            def f(e, k=k, wy=y_ms[k], wsi=si_y[k]):
                W(e, "si", wsi)
                W(e, "y", wy)
                e.collective_compute(
                    "AllGather", OP.bypass,
                    replica_groups=[[0, 1], [2, 3], [4, 5], [6, 7]],
                    ins=[ydr[k * 512:(k + 1) * 512, :].opt()],
                    outs=[ccd[k * 1024:(k + 1) * 1024, :].opt()],
                ).then_inc(SM["ccv"], 1)
                e.sem_inc(SM["gc"], 1)
            emit("g", f)
            ccv_ms[k] = res("ccv", 1)
            ccv_gc[k] = res("gc", 1)

        for step in range(NHC):
            if step >= 2:
                emit_ytr(2 * step - 4)
                emit_ytr(2 * step - 3)
                if (2 * step - 3) % 8 == 7:
                    emit_cc((2 * step - 3) // 8)
            emit_ffn_strips(step,
                            scan_sus=((2 * step - 2, 2 * step - 1)
                                      if step >= 1 else None))
            emit_ffn_m2(step)
            emit_scan_head(2 * step)
            emit_scan_head(2 * step + 1)
        emit_ytr(2 * NHC - 4)
        emit_ytr(2 * NHC - 3)
        for c in range(4):
            emit_scan_cluster_a(2 * NHC - 2, c)
            emit_scan_cluster(2 * NHC - 2, c)
        for c in range(4):
            emit_scan_cluster_a(2 * NHC - 1, c)
            emit_scan_cluster(2 * NHC - 1, c)
        emit_ytr(2 * NHC - 2)
        emit_ytr(2 * NHC - 1)
        emit_cc(3)

        ACC_TOTAL = CNT["acc"]
        ACC2_TOTAL = CNT["acc2"]
        ccd_flat = ccd.bitcast(U8).rearrange("a b -> (a b)")
        CB = MYCH * 2
        for i in range(2):
            def f(e, i=i, wca=(CNT["cca0"] - 0 if i else 0), wgc=ccv_gc[3]):
                rbC = persist_off(e, "syc4", 2048 * CB, 0)
                base = i * 1024 * CB
                win = ccd_flat[base: base + 2048 * CB + 1024 * CB]
                W(e, "gc", wgc)
                W(e, "ccv", 4)
                if i:
                    W(e, "cca0", wca)
                e.dma_start(
                    out=ccbig[:, :].rearrange("p (s f) -> p s f", s=8),
                    in_=win[ds(rbC, 1024 * CB)]
                        .rearrange("(s p f) -> p s f", s=8, f=CB).bitcast(BF16),
                ).then_inc(SM["ccb"], 16)
                e.sem_inc(SM["si"], 1)
            emit("sy", f)
            ccb_ms = res("ccb", 16)
            si_ccb = res("si", 1)

            for sseg in range(8):
                r, tb = sseg // 4, sseg % 4

                def f(e, i=i, r=r, tb=tb, sseg=sseg, wld=ccb_ms, wsi=si_ccb):
                    W(e, "si", wsi)
                    W(e, "ccb", wld)
                    W(e, "acc", ACC_TOTAL)
                    W(e, "acc2", ACC2_TOTAL)
                    W(e, "xasm", XASM)
                    e.dma_start(
                        out=out[i * 512 + tb * 128: i * 512 + (tb + 1) * 128,
                                r * 1024:(r + 1) * 1024],
                        in_=ccbig[:, sseg * MYCH:(sseg + 1) * MYCH],
                        accum_op=OP.add).then_inc(SM["cca0"], 16)
                emit("g", f)
                res("cca0", 16)

        if taps:
            def f(e, wv=CNT["v"], wa=CNT["a"], wg=GEB):
                W(e, "v", wv)
                W(e, "a", wa)
                W(e, "geB", wg)
                e.dma_start(out=o_xn, in_=xnfm[:, :]).then_inc(SM["tap"], 16)
                e.dma_start(out=o_cols, in_=cols[:, :]).then_inc(SM["tap"], 16)
                e.dma_start(out=o_carry, in_=carry[:, :]).then_inc(SM["tap"], 16)
                e.dma_start(out=o_dtx, in_=dtx[:, :]).then_inc(SM["tap"], 16)
                e.sem_inc(SM["si"], 1)
            emit("sy", f)
            TAPS = res("tap", 64)
            SI_TAP = res("si", 1)

        def f(e, c0=CNT["cca0"]):
            W(e, "acc", ACC_TOTAL)
            W(e, "acc2", ACC2_TOTAL)
            W(e, "cca0", c0)
            W(e, "xasm", XASM)
            if taps:
                W(e, "si", SI_TAP)
                W(e, "tap", TAPS)
        emit("g", f)

        @block.sync
        def _(e):
            for fn in SEQ["sy"]:
                fn(e)

        @block.vector
        def _(e):
            for fn in SEQ["v"]:
                fn(e)

        @block.scalar
        def _(e):
            for fn in SEQ["a"]:
                fn(e)

        @block.tensor
        def _(e):
            for fn in SEQ["t"]:
                fn(e)

        @block.gpsimd
        def _(e):
            for fn in SEQ["g"]:
                fn(e)

    return nc


def make_in_maps(inputs):
    x = np.asarray(inputs["L0_attn_x"], dtype=np.float32)
    W_x = np.asarray(inputs["L0_attn_W_x"], dtype=np.float32)
    W_dt = np.asarray(inputs["L0_attn_W_dt"], dtype=np.float32)
    b_dt = np.asarray(inputs["L0_attn_b_dt"], dtype=np.float32)
    A_log = np.asarray(inputs["L0_attn_A_log"], dtype=np.float32)
    Dv = np.asarray(inputs["L0_attn_D"], dtype=np.float32)
    w_router = np.asarray(inputs["L0_ffn_w_router"], dtype=np.float32)
    w_fc = np.asarray(inputs["L0_ffn_w_fc"], dtype=np.float32)
    w_proj = np.asarray(inputs["L0_ffn_w_proj"], dtype=np.float32)

    wdtb_full = np.concatenate([W_dt.T, b_dt[None, :]], axis=0)
    w2t = np.ascontiguousarray(w_proj.transpose(0, 2, 1))
    ident = np.eye(128, dtype=BF)
    w1t_t = w_fc.transpose(0, 2, 1)
    perm = {0: np.arange(2048),
            1: np.concatenate([np.arange(1024, 2048), np.arange(0, 1024)])}
    wxt_h = {hh: np.ascontiguousarray(W_x.T[perm[hh]]) for hh in (0, 1)}
    wrt_h = {hh: np.ascontiguousarray(w_router[:, perm[hh]]) for hh in (0, 1)}
    w1t_h = {hh: np.ascontiguousarray(w1t_t[:, perm[hh], :]) for hh in (0, 1)}

    in_maps = []
    for c in range(8):
        b, h = c // 2, c % 2
        ch = slice(h * 1024, (h + 1) * 1024)
        in_maps.append({
            "xb": np.ascontiguousarray(x[b]),
            "xbp": np.ascontiguousarray(x[b][:, perm[h]]),
            "wxt": wxt_h[h],
            "wdtb": np.ascontiguousarray(wdtb_full[:, ch]),
            "alog": np.ascontiguousarray(
                A_log[ch].reshape(8, 128, NST).transpose(1, 0, 2).reshape(128, 8 * NST)),
            "dre": np.ascontiguousarray(Dv[ch].reshape(8, 128).T),
            "wrt": wrt_h[h],
            "w1t": w1t_h[h],
            "w2t": w2t,
            "identd": ident,
        })
    return in_maps


def kernel(**inputs):
    nc = build_nc()
    in_maps = make_in_maps(inputs)
    res_ = run_bass_kernel_spmd(nc, in_maps, core_ids=list(range(8)))
    outp = np.empty((4, 2048, 2048), dtype=np.float32)
    for c in range(8):
        b, h = c // 2, c % 2
        outp[b, h * 1024:(h + 1) * 1024, :] = res_.results[c]["out"]
    return outp


# version 25 (final)
# speedup vs baseline: 1.0382x; 1.0382x over previous
import numpy as np
import ml_dtypes
import concourse.bass as bass
import concourse.mybir as mybir
from concourse.bass_utils import run_bass_kernel_spmd
from concourse.expressions import RuntimeValue
from contextlib import ExitStack

F32 = mybir.dt.float32
BF16 = mybir.dt.bfloat16
U8 = mybir.dt.uint8
AF = mybir.ActivationFunctionType
OP = mybir.AluOpType
AX = mybir.AxisListType
ds = bass.ds
BF = ml_dtypes.bfloat16

TOK, DM = 2048, 2048
MYTOK, MYCH = 1024, 1024
NST = 16
TS, NCHUNK, NG = 512, 4, 8
HC, NHC = 1024, 16
NTILE = 16
EPS = 1e-5

ENG_SEMS = ("v", "a", "t", "gc")
DMA_SEMS = ("const", "smallw", "xasm", "xt0", "xt1", "gcol", "geB", "bmcd",
            "bc", "y", "w1a", "w1b", "w2", "acc", "acc2", "ccv", "ccl", "cca0", "cca1",
            "tap", "si", "xnw", "xnr", "ccb")
ENGS = ("sy", "v", "a", "t", "g")


def build_nc(taps=False):
    nc = bass.Bass("TRN2", target_bir_lowering=False)

    xb = nc.dram_tensor("xb", [TOK, DM], F32, kind="ExternalInput").ap()
    xbp = nc.dram_tensor("xbp", [TOK, DM], F32, kind="ExternalInput").ap()
    wxt = nc.dram_tensor("wxt", [DM, 96], F32, kind="ExternalInput").ap()
    wdtb = nc.dram_tensor("wdtb", [65, MYCH], F32, kind="ExternalInput").ap()
    alog = nc.dram_tensor("alog", [128, NG * NST], F32, kind="ExternalInput").ap()
    dre = nc.dram_tensor("dre", [128, NG], F32, kind="ExternalInput").ap()
    wrt = nc.dram_tensor("wrt", [2, DM], F32, kind="ExternalInput").ap()
    w1t = nc.dram_tensor("w1t", [2, DM, 8192], F32, kind="ExternalInput").ap()
    w2t = nc.dram_tensor("w2t", [2, 8192, DM], F32, kind="ExternalInput").ap()
    identd = nc.dram_tensor("identd", [128, 128], BF16, kind="ExternalInput").ap()
    out = nc.dram_tensor("out", [MYTOK, DM], F32, kind="ExternalOutput").ap()
    if taps:
        o_xn = nc.dram_tensor("o_xn", [128, 16 * TOK], BF16, kind="ExternalOutput").ap()
        o_cols = nc.dram_tensor("o_cols", [128, 256], F32, kind="ExternalOutput").ap()
        o_carry = nc.dram_tensor("o_carry", [128, 128], F32, kind="ExternalOutput").ap()
        o_dtx = nc.dram_tensor("o_dtx", [65, TOK], BF16, kind="ExternalOutput").ap()
        o_ydr = nc.dram_tensor("o_ydr", [TOK, MYCH], BF16, kind="ExternalOutput").ap()
        o_geB = nc.dram_tensor("o_geB", [128, 2 * MYTOK], BF16, kind="ExternalOutput").ap()

    ydr_t = nc.dram_tensor("ydr", [TOK, MYCH], BF16)
    ccd_t = nc.dram_tensor("ccd", [2 * TOK, MYCH], BF16)
    ydr = ydr_t.ap()
    ccd = ccd_t.ap()
    gdr2 = nc.dram_tensor("gdr2", [128, 32], F32).ap()
    xb_flat = xb.bitcast(U8).rearrange("a b -> (a b)")
    bmcd = nc.dram_tensor("bmcd", [32, TOK], BF16).ap()
    xnd = nc.dram_tensor("xnd", [128, 16 * TOK * 2], U8).ap()

    SEQ = {k: [] for k in ENGS}
    CNT = {k: 0 for k in ENG_SEMS + DMA_SEMS}

    def emit(eng, fn):
        SEQ[eng].append(fn)

    def res(key, n):
        CNT[key] += n
        return CNT[key]

    RT = {}

    def get_base(e, key, mult):
        k2 = (key, mult)
        if k2 not in RT:
            pid = nc.partition_id([e.engine])
            r = e.alloc_register(f"hb_{key}_{mult}")
            e.reg_alu(r, pid, 2, OP.mod)
            e.reg_alu(r, r, mult, OP.mult)
            RT[k2] = r
        return RT[k2]

    def off_reg(e, key, idx=0):
        k2 = ("off", key, idx)
        if k2 not in RT:
            RT[k2] = e.alloc_register(f"off_{key}_{idx}")
        return RT[k2]

    def persist_off(e, key, mult, const):
        k2 = ("p", key, mult, const)
        if k2 not in RT:
            rb = get_base(e, key, mult)
            r = e.alloc_register(f"po_{key}_{const}")
            e.reg_alu(r, rb, const, OP.add)
            RT[k2] = RuntimeValue(r, min_val=const, max_val=mult + const)
        return RT[k2]

    def rv(reg, lo, hi):
        return RuntimeValue(reg, min_val=lo, max_val=hi)

    with ExitStack() as ctx:
        def sb(nm, shp, dt):
            return ctx.enter_context(nc.sbuf_tensor(nm, shp, dt))

        def psum(nm, shp, dt):
            return ctx.enter_context(nc.psum_tensor(nm, shp, dt))

        xnfm = sb("xnfm", [128, 16 * TOK], BF16)
        xnu8 = xnfm[:, :].bitcast(U8)
        dtx = sb("dtx", [65, TOK], BF16)
        wdtbs = sb("wdtbs", [65, MYCH], BF16)
        Asb = sb("Asb", [128, NG * NST], F32)
        Dsb = sb("Dsb", [128, NG], F32)
        idn = sb("idn", [128, 128], BF16)
        carry = sb("carry", [128, 128], F32)
        gtok = sb("gtok", [128, 16], F32)
        stage2 = sb("stage2", [128, DM], BF16)
        cols = sb("cols", [128, 16 * 16], F32)

        (C_S1, C_S2, C_SXW0, C_SXW1, C_M, C_VPS, C_RSTD, C_NMR, C_D, C_SIG,
         C_GE, C_G0, C_G1, C_T0, C_T1, C_T2) = range(16)

        def col(idx, i):
            return cols[:, idx * 16 + i: idx * 16 + i + 1]

        def colsl(idx):
            return cols[:, idx * 16: idx * 16 + 16]

        UB = 134144
        union = sb("union", [128, UB], U8)

        def carve(off_bytes, nelem, dt):
            esz = {F32: 4, BF16: 2}[dt]
            return union[:, off_bytes: off_bytes + nelem * esz].bitcast(dt)

        class Cursor:
            def __init__(self):
                self.o = 0

            def take(self, nelem, dt):
                esz = {F32: 4, BF16: 2}[dt]
                ap = carve(self.o, nelem, dt)
                self.o += nelem * esz
                self.o = (self.o + 15) & ~15
                return ap

        cu = Cursor()
        w1pp = [cu.take(16 * 128, BF16), cu.take(16 * 128, BF16)]
        w2c = cu.take(8 * DM, BF16)
        aslot = cu.take(8 * MYTOK, BF16)
        stg = [cu.take(DM, BF16), None]
        fbuf = [cu.take(512, BF16) for _ in range(3)]
        sqb = [cu.take(512, BF16)]
        BmBb = cu.take(NST * TS, BF16)
        CmBb = cu.take(NST * TS, BF16)
        dtg = [cu.take(TS, F32), cu.take(TS, F32)]
        ug = [cu.take(TS, BF16), cu.take(TS, BF16)]
        dab = [cu.take(TS, BF16) for _ in range(4)]
        dbxb = [cu.take(TS, BF16) for _ in range(4)]
        hball = cu.take(4 * TS, BF16)
        hb = [hball[:, i * TS:(i + 1) * TS] for i in range(4)]
        hC = [cu.take(TS, BF16) for _ in range(4)]
        ypb = cu.take(TS, BF16)
        ybuf2 = [cu.take(TS, BF16), cu.take(TS, BF16)]
        yst = [cu.take(MYCH, BF16) for _ in range(4)]
        t1s = cu.take(TS, F32)
        assert cu.o <= UB, cu.o

        cu = Cursor()
        xt = [cu.take(DM, F32), cu.take(DM, F32)]
        wrB = [cu.take(DM, F32), cu.take(DM, F32)]
        sqscr = cu.take(DM, BF16)
        sqg = cu.take(DM, BF16)
        bmcm = cu.take(TOK, F32)
        xnt = [cu.take(DM, BF16), cu.take(DM, BF16)]
        xasm = cu.take(DM, F32)
        wxts = cu.take(16 * 96, BF16)
        swc = cu.take(2, F32)
        assert cu.o <= UB, cu.o

        ccbig = carve(72 * 1024, 8 * MYCH, BF16)

        stg[1] = stage2[:, :]
        pm1 = [psum("pm1a", [128, 512], F32), psum("pm1b", [128, 512], F32),
               psum("pm1c", [128, 512], F32)]
        pm2 = [psum("pm2a", [128, 512], F32), psum("pm2b", [128, 512], F32)]
        pdt = psum("pdt", [128, 512], F32)
        ptr = [psum("ptra", [128, 512], BF16), psum("ptrb", [128, 512], BF16)]

        SM = {k: ctx.enter_context(nc.semaphore(f"s_{k}"))
              for k in ENG_SEMS + DMA_SEMS}
        block = ctx.enter_context(nc.Block())

        def W(e, key, val):
            if val:
                e.wait_ge(SM[key], val)

        class Ser:

            def __init__(self, key):
                self.key = key
                self.base = CNT[key]
                self.n = 0

            def op(self):
                w = self.base + self.n
                self.n += 1
                res(self.key, 1)
                return w

        def f(e):
            e.dma_start(out=idn[:, :], in_=identd).then_inc(SM["const"], 16)
            e.dma_start(out=Asb[:, :], in_=alog).then_inc(SM["const"], 16)
            e.dma_start(out=Dsb[:, :], in_=dre).then_inc(SM["const"], 16)
            e.dma_start(out=wrB[0][:, :].rearrange("p (o f) -> p o f", o=1),
                        in_=wrt[0:1, :].partition_broadcast(128)).then_inc(SM["const"], 16)
            e.dma_start(out=wrB[1][:, :].rearrange("p (o f) -> p o f", o=1),
                        in_=wrt[1:2, :].partition_broadcast(128)).then_inc(SM["const"], 16)
            e.sem_inc(SM["si"], 1)
        emit("sy", f)
        CONSTS = res("const", 5 * 16)
        SI_CONST = res("si", 1)

        def f(e):
            e.dma_start(out=wxts[:, :].rearrange("p (a m) -> p a m", a=16),
                        in_=wxt.rearrange("(a p) m -> p a m", p=128)).then_inc(SM["smallw"], 16)
            e.dma_start(out=wdtbs[:, :], in_=wdtb).then_inc(SM["smallw"], 16)
            e.sem_inc(SM["gc"], 1)
        emit("g", f)
        SMALLW = res("smallw", 2 * 16)
        GC_SMALLW = res("gc", 1)

        for tt in range(8):
            w0 = CNT["xasm"]

            def f(e, tt=tt, w0=w0):
                rbX = persist_off(e, "syx", 1024 * DM * 4, 0)
                base = tt * 128 * DM * 4
                win = xb_flat[base: base + 1024 * DM * 4 + 128 * DM * 4]
                W(e, "xasm", w0)
                e.dma_start(out=xasm[:, :],
                            in_=win[ds(rbX, 128 * DM * 4)]
                                .rearrange("(p f) -> p f", f=DM * 4).bitcast(F32)
                            ).then_inc(SM["xasm"], 16)
                W(e, "xasm", w0 + 16)
                e.dma_start(out=out[tt * 128:(tt + 1) * 128, :],
                            in_=xasm[:, :]).then_inc(SM["xasm"], 16)
            emit("sy", f)
            res("xasm", 32)
        XASM = CNT["xasm"]

        sa = Ser("a")
        wA0 = sa.op()

        def f(e, w=wA0):
            W(e, "si", SI_CONST)
            W(e, "const", CONSTS)
            W(e, "a", w)
            e.activation(Asb[:, :], Asb[:, :], AF.Exp).then_inc(SM["a"], 1)
        emit("a", f)
        a_Aexp = CNT["a"]

        sv = Ser("v")
        wV0 = sv.op()

        def f(e, w=wV0, wa=a_Aexp):
            W(e, "a", wa)
            W(e, "v", w)
            e.tensor_scalar(Asb[:, :], Asb[:, :], -1.0, None, OP.mult).then_inc(SM["v"], 1)
        emit("v", f)
        v_Aneg = CNT["v"]

        vWAR = [0] * NTILE
        aWAR = [0] * NTILE
        gWAR = [0] * NTILE
        t_tr = {}
        a_ev = {}

        for i in range(NTILE):
            xk = "xt%d" % (i % 2)

            def f(e, i=i, xk=xk, wv=(vWAR[i - 2] if i >= 2 else 0),
                  wa=(aWAR[i - 2] if i >= 2 else 0),
                  wg=0):
                W(e, "v", wv)
                W(e, "a", wa)
                e.dma_start(out=xt[i % 2][:, :],
                            in_=xbp[i * 128:(i + 1) * 128, :]).then_inc(SM[xk], 16)
                e.sem_inc(SM["si"], 1)
            emit("sy", f)
            xt_ms = res(xk, 16)
            si_xt = res("si", 1)

            s = Ser("v")
            ws = [s.op() for _ in range(9)]

            def f(e, i=i, xk=xk, wx=xt_ms, ws=ws, wsi=si_xt):
                W(e, "si", wsi)
                W(e, xk, wx)
                W(e, "const", CONSTS)
                x = xt[i % 2][:, :]
                W(e, "v", ws[0])
                e.tensor_reduce(col(C_S1, i), x, AX.X, OP.add).then_inc(SM["v"], 1)
                W(e, "v", ws[1])
                e.scalar_tensor_tensor(sqscr[:, :], x, 1.0, x, OP.mult, OP.mult,
                                       accum_out=col(C_S2, i)).then_inc(SM["v"], 1)
                W(e, "v", ws[2])
                e.scalar_tensor_tensor(sqscr[:, :], x, 1.0, wrB[0][:, :], OP.mult, OP.mult,
                                       accum_out=col(C_SXW0, i)).then_inc(SM["v"], 1)
                W(e, "v", ws[3])
                e.scalar_tensor_tensor(sqscr[:, :], x, 1.0, wrB[1][:, :], OP.mult, OP.mult,
                                       accum_out=col(C_SXW1, i)).then_inc(SM["v"], 1)
                W(e, "v", ws[4])
                e.tensor_scalar(col(C_M, i), col(C_S1, i), 1.0 / DM, None, OP.mult).then_inc(SM["v"], 1)
                W(e, "v", ws[5])
                e.tensor_scalar(col(C_T0, i), col(C_S2, i), 1.0 / DM, None, OP.mult).then_inc(SM["v"], 1)
                W(e, "v", ws[6])
                e.tensor_tensor(col(C_T1, i), col(C_M, i), col(C_M, i), OP.mult).then_inc(SM["v"], 1)
                W(e, "v", ws[7])
                e.tensor_tensor(col(C_T2, i), col(C_T0, i), col(C_T1, i), OP.subtract).then_inc(SM["v"], 1)
                W(e, "v", ws[8])
                e.tensor_scalar(col(C_VPS, i), col(C_T2, i), EPS, None, OP.add).then_inc(SM["v"], 1)
            emit("v", f)
            v_stats = CNT["v"]

            s = Ser("a")
            ws = [s.op() for _ in range(2)]

            def f(e, i=i, wv=v_stats, ws=ws):
                W(e, "v", wv)
                W(e, "a", ws[0])
                e.activation(col(C_T0, i), col(C_VPS, i), AF.Ln).then_inc(SM["a"], 1)
                W(e, "a", ws[1])
                e.activation(col(C_RSTD, i), col(C_T0, i), AF.Exp, scale=-0.5).then_inc(SM["a"], 1)
            emit("a", f)
            a_rstd = CNT["a"]

            s = Ser("v")
            ws = [s.op() for _ in range(2)]

            def f(e, i=i, wa=a_rstd, ws=ws):
                W(e, "a", wa)
                W(e, "v", ws[0])
                e.tensor_tensor(col(C_T1, i), col(C_M, i), col(C_RSTD, i), OP.mult).then_inc(SM["v"], 1)
                W(e, "v", ws[1])
                e.tensor_scalar(col(C_NMR, i), col(C_T1, i), -1.0, None, OP.mult).then_inc(SM["v"], 1)
            emit("v", f)
            vWAR[i] = CNT["v"]

            s = Ser("a")
            w1_ = s.op()

            def f(e, i=i, wv=vWAR[i], w1_=w1_):
                W(e, "v", wv)
                W(e, "a", w1_)
                e.activation(xnt[i % 2][:, :], xt[i % 2][:, :], AF.Identity,
                             bias=col(C_NMR, i), scale=col(C_RSTD, i)).then_inc(SM["a"], 1)
            emit("a", f)
            aWAR[i] = CNT["a"]

            for j in range(16):
                li = i * 16 + j

                def f(e, i=i, j=j, li=li, wa=max(aWAR[i], a_ev.get(li - 2, 0))):
                    if li == 0:
                        W(e, "const", CONSTS)
                    W(e, "a", wa)
                    e.transpose(ptr[j % 2][:, 0:128],
                                xnt[i % 2][:, j * 128:(j + 1) * 128],
                                idn[:, :]).then_inc(SM["t"], 1)
                emit("t", f)
                t_tr[li] = res("t", 1)

                s = Ser("a")
                w1_ = s.op()

                def f(e, i=i, j=j, wt=t_tr[li], w1_=w1_):
                    W(e, "t", wt)
                    W(e, "a", w1_)
                    e.activation(xnfm[:, j * TOK + i * 128: j * TOK + (i + 1) * 128],
                                 ptr[j % 2][:, 0:128], AF.Identity).then_inc(SM["a"], 1)
                emit("a", f)
                a_ev[li] = CNT["a"]

        xn_done_a = CNT["a"]

        s = Ser("v")
        ws = [s.op() for _ in range(10)]

        def f(e, ws=ws):
            ops = [
                lambda: e.tensor_reduce(swc[:, 0:1], wrB[0][:, :], AX.X, OP.add),
                lambda: e.tensor_reduce(swc[:, 1:2], wrB[1][:, :], AX.X, OP.add),
                lambda: e.tensor_tensor(colsl(C_T0), colsl(C_M),
                                        swc[:, 0:1].broadcast_to([128, 16]), OP.mult),
                lambda: e.tensor_tensor(colsl(C_T1), colsl(C_SXW0), colsl(C_T0), OP.subtract),
                lambda: e.tensor_tensor(colsl(C_G0), colsl(C_T1), colsl(C_RSTD), OP.mult),
                lambda: e.tensor_tensor(colsl(C_T0), colsl(C_M),
                                        swc[:, 1:2].broadcast_to([128, 16]), OP.mult),
                lambda: e.tensor_tensor(colsl(C_T1), colsl(C_SXW1), colsl(C_T0), OP.subtract),
                lambda: e.tensor_tensor(colsl(C_G1), colsl(C_T1), colsl(C_RSTD), OP.mult),
                lambda: e.tensor_tensor(colsl(C_D), colsl(C_G0), colsl(C_G1), OP.subtract),
            ]
            for w, op_ in zip(ws, ops):
                W(e, "v", w)
                op_().then_inc(SM["v"], 1)
        emit("v", f)
        v_dcol = CNT["v"] - 1
        res("v", 0)
        CNT["v"] -= 1
        v_dcol = CNT["v"]

        s = Ser("a")
        w1_ = s.op()

        def f(e, wv=v_dcol, w1_=w1_):
            W(e, "v", wv)
            W(e, "a", w1_)
            e.activation(colsl(C_SIG), colsl(C_D), AF.Sigmoid).then_inc(SM["a"], 1)
        emit("a", f)
        a_sig = CNT["a"]

        s = Ser("v")
        ws = [s.op() for _ in range(5)]

        def f(e, wa=a_sig, ws=ws):
            W(e, "a", wa)
            ops = [
                lambda: e.tensor_scalar(colsl(C_GE), colsl(C_D), 0.0, None, OP.is_ge),
                lambda: e.tensor_tensor(colsl(C_G0), colsl(C_SIG), colsl(C_GE), OP.mult),
                lambda: e.tensor_scalar(colsl(C_T0), colsl(C_SIG), -1.0, 1.0, OP.mult, OP.add),
                lambda: e.tensor_scalar(colsl(C_T1), colsl(C_GE), -1.0, 1.0, OP.mult, OP.add),
                lambda: e.tensor_tensor(colsl(C_G1), colsl(C_T0), colsl(C_T1), OP.mult),
            ]
            for w, op_ in zip(ws, ops):
                W(e, "v", w)
                op_().then_inc(SM["v"], 1)
        emit("v", f)
        v_g = CNT["v"]

        def f(e, w=v_g):
            W(e, "v", w)
            for hh in range(2):
                e.dma_start(out=gdr2[:, hh * 16: hh * 16 + 8],
                            in_=cols[:, C_G0 * 16 + hh * 8: C_G0 * 16 + hh * 8 + 8]
                            ).then_inc(SM["gcol"], 16)
                e.dma_start(out=gdr2[:, hh * 16 + 8: hh * 16 + 16],
                            in_=cols[:, C_G1 * 16 + hh * 8: C_G1 * 16 + hh * 8 + 8]
                            ).then_inc(SM["gcol"], 16)
            e.sem_inc(SM["si"], 1)
        emit("sy", f)
        GCOL = res("gcol", 64)
        SI_GCOL = res("si", 1)

        def f(e):
            rbx = persist_off(e, "ggx", 16, 0)
            W(e, "si", SI_GCOL)
            W(e, "gcol", GCOL)
            e.dma_start(out=gtok[:, 0:16],
                        in_=gdr2[:, ds(rbx, 16)]).then_inc(SM["geB"], 16)
            e.sem_inc(SM["si"], 1)
        emit("sy", f)
        GEB = res("geB", 16)
        SI_GEB = res("si", 1)

        xp_a = {}
        xp_v = {}
        t_xp = {}
        for tc in range(4):
            def f(e, tc=tc, wvv=xp_v.get(tc - 2, 0),
                  wa=max(xn_done_a, xp_a.get(tc - 2, 0))):
                W(e, "gc", GC_SMALLW)
                W(e, "smallw", SMALLW)
                W(e, "v", wvv)
                W(e, "a", wa)
                p = pm1[tc % 2]
                for dc in range(16):
                    mm = e.matmul(p[0:96, :], wxts[:, dc * 96:(dc + 1) * 96],
                                  xnfm[:, dc * TOK + tc * 512: dc * TOK + (tc + 1) * 512],
                                  start=(dc == 0), stop=(dc == 15))
                mm.then_inc(SM["t"], 1)
            emit("t", f)
            t_xp[tc] = res("t", 1)

            s = Ser("a")
            w1_ = s.op()

            def f(e, tc=tc, wt=t_xp[tc], w1_=w1_):
                W(e, "t", wt)
                W(e, "a", w1_)
                e.activation(dtx[0:64, tc * 512:(tc + 1) * 512],
                             pm1[tc % 2][0:64, :], AF.Identity).then_inc(SM["a"], 1)
            emit("a", f)
            xp_a[tc] = CNT["a"]

            s = Ser("v")
            w1_ = s.op()

            def f(e, tc=tc, wt=t_xp[tc], w1_=w1_):
                W(e, "t", wt)
                W(e, "v", w1_)
                e.tensor_copy(bmcm[64:96, tc * 512:(tc + 1) * 512],
                              pm1[tc % 2][64:96, :]).then_inc(SM["v"], 1)
            emit("v", f)
            xp_v[tc] = CNT["v"]

        s = Ser("v")
        w1_ = s.op()

        def f(e, w=xp_v[3], w1_=w1_):
            W(e, "v", max(w, w1_))
            e.memset(dtx[64:65, :], 1.0).then_inc(SM["v"], 1)
        emit("v", f)
        v_ones = CNT["v"]

        def f(e, w=v_ones):
            W(e, "v", w)
            e.dma_start(out=bmcd, in_=bmcm[64:96, :]).then_inc(SM["bmcd"], 16)
            e.sem_inc(SM["gc"], 1)
        emit("g", f)
        BMCD = res("bmcd", 16)
        GC_BMCD = res("gc", 1)

        def f(e, wa=xn_done_a):
            W(e, "a", wa)
            e.dma_start(out=xnd, in_=xnu8).then_inc(SM["xnw"], 16)
            e.sem_inc(SM["si"], 1)
        emit("sy", f)
        XNW = res("xnw", 16)
        SI_XNW = res("si", 1)

        def f(e, wt=t_xp[3]):
            rbB = persist_off(e, "sy", 2048, 0)
            W(e, "si", SI_XNW)
            W(e, "xnw", XNW)
            W(e, "t", wt)
            e.dma_start(
                out=xnu8[:, 8 * 4096: 16 * 4096].rearrange("p (a f) -> p a f", a=16),
                in_=xnd.rearrange("p (a f) -> p a f", a=16)[:, :, ds(rbB, 2048)],
            ).then_inc(SM["xnr"], 16)
            e.sem_inc(SM["si"], 1)
        emit("sy", f)
        XNR = res("xnr", 16)
        SI_XNR = res("si", 1)

        EARLY = dict(v=v_ones, a=xp_a[3], t=t_xp[3], gc=GC_BMCD,
                     si=max(SI_XNR, SI_GEB))
        EARLY_DMA = dict(const=CONSTS, smallw=SMALLW, xasm=XASM, bmcd=BMCD,
                         gcol=GCOL, geB=GEB, xnr=XNR)

        for eng in ENGS:
            def f(e, eng=eng):
                for k2, val in EARLY.items():
                    W(e, k2, val)
                for k2, val in EARLY_DMA.items():
                    W(e, k2, val)
            emit(eng, f)

        ffn_acc = {}
        a_ev2 = {}
        v_stt = {}
        v_am = {}
        a_sq = {}
        a_ev1 = {}
        t_m1 = {}
        t_m2 = {}
        t_m2l = {}
        v_scan = {}
        g_hC = {}
        g_dbx = {}
        a_dA = {}
        v_dtg = {}
        a_sp = {}
        g_tree = {}
        g_ypb = {}
        v_ybuf = {}
        t_ytr = {}
        a_yev = {}
        y_ms = {}
        bc_ms = {}
        w1_ms = {}
        w2_ms = {}
        w1_gc = {}
        w2_gc = {}
        acc_gc = {}
        si_bc = {}
        si_y = {}

        NSTRIPS = NHC * 8
        YPC = 128 if taps else 64

        def w1_load(strip):
            wk = "w1%s" % ("ab"[strip % 2])

            def f(e, strip=strip, wk=wk, wt=t_m1.get(strip - 2, 0)):
                W(e, "t", wt)
                hc, sидx = strip // 8, strip % 8
                e_idx = hc // 8
                c0 = (hc % 8) * HC + sидx * 128
                e.dma_start(
                    out=w1pp[strip % 2][:, :].rearrange("p (a f) -> p a f", a=16),
                    in_=w1t[e_idx:e_idx + 1, :, c0:c0 + 128].rearrange(
                        "o (a p) f -> p (o a) f", p=128),
                ).then_inc(SM[wk], 16)
                e.sem_inc(SM["gc"], 1)
            emit("g", f)
            w1_ms[strip] = res(wk, 16)
            w1_gc[strip] = res("gc", 1)

        def w2_load(hc):
            def f(e, hc=hc, wt=t_m2.get(hc - 1, 0)):
                W(e, "t", wt)
                e_idx = hc // 8
                r0 = (hc % 8) * HC
                e.dma_start(
                    out=w2c[:, :].rearrange("p (a f) -> p a f", a=8),
                    in_=w2t[e_idx:e_idx + 1, r0:r0 + HC, :].rearrange(
                        "o (a p) f -> p (o a) f", p=128),
                ).then_inc(SM["w2"], 16)
                e.sem_inc(SM["gc"], 1)
            emit("g", f)
            w2_ms[hc] = res("w2", 16)
            w2_gc[hc] = res("gc", 1)

        w1_load(0)
        w1_load(1)
        w2_load(0)

        def emit_ffn_strips(hc, scan_sus=None):
            e_idx = hc // 8
            for sx in range(8):
                strip = hc * 8 + sx
                wk = "w1%s" % ("ab"[strip % 2])

                t_m1h = {}
                for tt in range(2):
                    q = strip * 2 + tt

                    def f(e, strip=strip, tt=tt, q=q, wk=wk, wg=w1_ms[strip],
                          wgc=w1_gc[strip], wv=a_ev1.get(q - 3, 0)):
                        if tt == 0:
                            W(e, "gc", wgc)
                            W(e, wk, wg)
                        W(e, "a", wv)
                        for dc in range(16):
                            base = 8 * 4096 + dc * 2048 + tt * 1024
                            mm = e.matmul(pm1[q % 3][:, :],
                                          w1pp[strip % 2][:, dc * 128:(dc + 1) * 128],
                                          xnu8[:, base: base + 1024].bitcast(BF16),
                                          start=(dc == 0), stop=(dc == 15))
                        mm.then_inc(SM["t"], 1)
                    emit("t", f)
                    t_m1h[q] = res("t", 1)
                t_m1[strip] = t_m1h[strip * 2 + 1]
                if strip + 2 < NSTRIPS:
                    w1_load(strip + 2)

                if sx >= 1:
                    emit_act_chain(hc, strip - 1)
                if scan_sus is not None:
                    emit_scan_cluster_a(scan_sus[sx // 4], sx % 4)

                for tt in range(2):
                    q = strip * 2 + tt

                    s = Ser("a")
                    w1_ = s.op()

                    def f(e, strip=strip, tt=tt, q=q, wt=t_m1h[q], w1_=w1_):
                        W(e, "t", wt)
                        W(e, "a", w1_)
                        e.activation(fbuf[q % 3][:, :], pm1[q % 3][:, :],
                                     AF.Identity).then_inc(SM["a"], 1)
                    emit("a", f)
                    a_ev1[q] = CNT["a"]

                if sx == 7:
                    emit_act_chain(hc, strip)
                if scan_sus is not None:
                    emit_scan_cluster(scan_sus[sx // 4], sx % 4)

        def emit_act_chain(hc, strip):
            e_idx = hc // 8
            sx = strip % 8
            for tt in range(2):
                q = strip * 2 + tt

                s = Ser("v")
                w1_ = s.op()

                def f(e, q=q, wa=a_ev1[q], w1_=w1_, wsq=a_sq.get(q - 1, 0)):
                    W(e, "a", max(wa, wsq))
                    W(e, "v", w1_)
                    f0 = fbuf[q % 3][:, :]
                    e.scalar_tensor_tensor(sqb[0][:, :], f0, 0.5, f0,
                                           OP.mult, OP.max).then_inc(SM["v"], 1)
                emit("v", f)
                v_stt[q] = CNT["v"]

                s = Ser("a")
                w1_ = s.op()

                def f(e, sx=sx, tt=tt, q=q, wv=v_stt[q], w1_=w1_):
                    W(e, "v", wv)
                    W(e, "a", w1_)
                    e.activation(
                        aslot[:, sx * MYTOK + tt * 512: sx * MYTOK + (tt + 1) * 512],
                        sqb[0][:, :], AF.Square).then_inc(SM["a"], 1)
                emit("a", f)
                a_sq[q] = CNT["a"]

        def emit_ffn_m2(hc):
            a_ready = a_sq[(hc * 8 + 7) * 2 + 1]
            for tt in range(8):
                for dq in range(4):
                    m = (hc * 8 + tt) * 4 + dq

                    def f(e, hc=hc, tt=tt, dq=dq, m=m, wv=a_ready,
                          wg=w2_ms[hc], wgc=w2_gc[hc], wae=a_ev2.get(m - 2, 0)):
                        W(e, "a", wv)
                        W(e, "gc", wgc)
                        W(e, "w2", wg)
                        W(e, "a", wae)
                        p = pm2[m % 2]
                        for sx in range(8):
                            mm = e.matmul(p[:, :],
                                          aslot[:, sx * MYTOK + tt * 128: sx * MYTOK + tt * 128 + 128],
                                          w2c[:, sx * DM + dq * 512: sx * DM + (dq + 1) * 512],
                                          start=(sx == 0), stop=(sx == 7))
                        mm.then_inc(SM["t"], 1)
                    emit("t", f)
                    t_m2l[m] = res("t", 1)

                    s = Ser("a")
                    w1_ = s.op()

                    def f(e, hc=hc, tt=tt, dq=dq, m=m, wt=t_m2l[m], w1_=w1_,
                          wgacc=acc_gc.get(hc * 8 + tt - 2, 0)):
                        L = hc * 8 + tt
                        W(e, "t", wt)
                        if dq == 0 and wgacc:
                            W(e, "gc", wgacc)
                            W(e, "acc" if L % 2 == 0 else "acc2",
                              16 * (L // 2))
                        W(e, "a", w1_)
                        e.activation(stg[L % 2][:, dq * 512:(dq + 1) * 512],
                                     pm2[m % 2][:, :], AF.Identity,
                                     scale=gtok[:, (hc // 8) * 8 + tt:
                                                (hc // 8) * 8 + tt + 1]).then_inc(SM["a"], 1)
                    emit("a", f)
                    a_ev2[m] = CNT["a"]

                L = hc * 8 + tt

                def f(e, hc=hc, tt=tt, L=L, wa=a_ev2[(hc * 8 + tt) * 4 + 3]):
                    W(e, "a", wa)
                    W(e, "xasm", XASM)
                    e.dma_start(out=out[tt * 128:(tt + 1) * 128, :], in_=stg[L % 2][:, :],
                                accum_op=OP.add).then_inc(
                                    SM["acc" if L % 2 == 0 else "acc2"], 16)
                    e.sem_inc(SM["gc"], 1)
                emit("g", f)
                ffn_acc[L] = res("acc" if L % 2 == 0 else "acc2", 16)
                acc_gc[L] = res("gc", 1)

            t_m2[hc] = t_m2l[(hc * 8 + 7) * 4 + 3]
            if hc + 1 < NHC:
                w2_load(hc + 1)

        def emit_tree(su, qq):
            s = Ser("v")
            ws = [s.op() for _ in range(4)]

            def f(e, su=su, qq=qq, ws=ws):
                W(e, "v", ws[0])
                e.tensor_tensor(hC[0][:, :], hC[0][:, :], hC[1][:, :], OP.add).then_inc(SM["v"], 1)
                W(e, "v", ws[1])
                e.tensor_tensor(hC[2][:, :], hC[2][:, :], hC[3][:, :], OP.add).then_inc(SM["v"], 1)
                W(e, "v", ws[2])
                e.tensor_tensor(hC[0][:, :], hC[0][:, :], hC[2][:, :], OP.add).then_inc(SM["v"], 1)
                W(e, "v", ws[3])
                if qq == 0:
                    e.tensor_copy(ypb[:, :], hC[0][:, :]).then_inc(SM["v"], 1)
                else:
                    e.tensor_tensor(ypb[:, :], ypb[:, :], hC[0][:, :],
                                    OP.add).then_inc(SM["v"], 1)
            emit("v", f)

        def emit_scan_head(su):
            k, gi = su // 8, su % 8
            pp = su % 2

            if gi == 0:
                def f(e, k=k,
                      wv=v_ybuf.get(su - 1, 0)):
                    W(e, "bmcd", BMCD)
                    W(e, "v", wv)
                    e.dma_start(out=BmBb[:, :].rearrange("p (n t) -> p n t", n=NST),
                                in_=bmcd[0:16, k * TS:(k + 1) * TS].partition_broadcast(128)
                                ).then_inc(SM["bc"], 16)
                    e.dma_start(out=CmBb[:, :].rearrange("p (n t) -> p n t", n=NST),
                                in_=bmcd[16:32, k * TS:(k + 1) * TS].partition_broadcast(128)
                                ).then_inc(SM["bc"], 16)
                    e.sem_inc(SM["si"], 1)
                emit("sy", f)
                bc_ms[k] = res("bc", 32)
                si_bc[k] = res("si", 1)

            def f(e, su=su, k=k, gi=gi, wfree=v_dtg.get(su - 1, 0)):
                W(e, "v", wfree)
                e.matmul(pdt[:, :], wdtbs[:, gi * 128:(gi + 1) * 128],
                         dtx[:, k * TS:(k + 1) * TS], start=True, stop=True).then_inc(SM["t"], 1)
            emit("t", f)
            t_dt = res("t", 1)

            s = Ser("a")
            ws = [s.op() for _ in range(3)]

            def f(e, su=su, wt=t_dt, wfree=v_dtg.get(su - 1, 0), ws=ws):
                W(e, "t", wt)
                W(e, "v", wfree)
                W(e, "a", ws[0])
                e.activation(t1s[:, :], pdt[:, :], AF.Abs).then_inc(SM["a"], 1)
                W(e, "a", ws[1])
                e.activation(t1s[:, :], t1s[:, :], AF.Exp, scale=-1.0).then_inc(SM["a"], 1)
                W(e, "a", ws[2])
                e.activation(t1s[:, :], t1s[:, :], AF.Ln, bias=1.0).then_inc(SM["a"], 1)
            emit("a", f)
            a_sp[su] = CNT["a"]

            s = Ser("v")
            ws = [s.op() for _ in range(3)]

            def f(e, su=su, k=k, gi=gi, pp=pp, wt=t_dt, wa=a_sp[su],
                  wdA=a_dA.get((su - 2, NST - 1), 0), ws=ws):
                xnsl = xnu8[:, gi * 4096 + k * 1024: gi * 4096 + (k + 1) * 1024]
                W(e, "t", wt)
                W(e, "a", max(wa, wdA))
                W(e, "v", ws[0])
                e.tensor_scalar(dtg[pp][:, :], pdt[:, :], 0.0, None, OP.max).then_inc(SM["v"], 1)
                W(e, "v", ws[1])
                e.tensor_tensor(dtg[pp][:, :], dtg[pp][:, :], t1s[:, :], OP.add).then_inc(SM["v"], 1)
                W(e, "v", ws[2])
                e.tensor_tensor(ug[pp][:, :], dtg[pp][:, :],
                                xnsl.bitcast(BF16),
                                OP.mult).then_inc(SM["v"], 1)
            emit("v", f)
            v_dtg[su] = CNT["v"]

        def emit_scan_cluster_a(su, c):
            gi = su % 8
            for n in range(4 * c, 4 * c + 4):
                tile = gi * NST + n
                wfree_scan = (v_scan.get((su, n - 4), 0) if n >= 4
                              else v_scan.get((su - 1, NST - 4 + n), 0))

                s = Ser("a")
                w1_ = s.op()

                def f(e, su=su, n=n, tile=tile, wv=v_dtg[su],
                      wfree=wfree_scan, wA=v_Aneg, w1_=w1_):
                    W(e, "v", max(wv, wA, wfree))
                    W(e, "a", w1_)
                    e.activation(dab[n % 4][:, :], dtg[su % 2][:, :], AF.Exp,
                                 scale=Asb[:, tile:tile + 1]).then_inc(SM["a"], 1)
                emit("a", f)
                a_dA[(su, n)] = CNT["a"]

        def emit_scan_cluster(su, c):
            k, gi = su // 8, su % 8

            for n in range(4 * c, 4 * c + 4):
                tile = gi * NST + n
                npp = n % 4

                s = Ser("v")
                ws = [s.op() for _ in range(3)]

                def f(e, su=su, k=k, n=n, npp=npp, tile=tile,
                      wa=a_dA[(su, n)], ws=ws):
                    W(e, "si", si_bc[k])
                    W(e, "bc", bc_ms[k])
                    W(e, "a", wa)
                    init = 0.0 if k == 0 else carry[:, tile:tile + 1]
                    W(e, "v", ws[0])
                    e.tensor_tensor(dbxb[npp][:, :], ug[su % 2][:, :],
                                    BmBb[:, n * TS:(n + 1) * TS], OP.mult).then_inc(SM["v"], 1)
                    W(e, "v", ws[1])
                    e.tensor_tensor_scan(hb[npp][:, :], dab[n % 4][:, :], dbxb[npp][:, :],
                                         init, OP.mult, OP.add).then_inc(SM["v"], 1)
                    W(e, "v", ws[2])
                    e.tensor_tensor(hC[n % 4][:, :], hb[npp][:, :],
                                    CmBb[:, n * TS:(n + 1) * TS], OP.mult).then_inc(SM["v"], 1)
                emit("v", f)
                v_scan[(su, n)] = CNT["v"]

            if su < 24:
                t0 = gi * NST + 4 * c

                s = Ser("v")
                w1_ = s.op()

                def f(e, t0=t0, w1_=w1_):
                    W(e, "v", w1_)
                    e.tensor_copy(
                        carry[:, t0:t0 + 4].rearrange("p (s o) -> p s o", o=1),
                        hball[:, :].rearrange("p (s t) -> p s t", s=4)[:, :, TS - 1:TS],
                    ).then_inc(SM["v"], 1)
                emit("v", f)

            emit_tree(su, c)
            if c == 3:
                k, gi = su // 8, su % 8

                s = Ser("v")
                w1_ = s.op()

                def f(e, su=su, k=k, gi=gi, wt=t_ytr.get(su - 2, 0), w1_=w1_):
                    xnsl = xnu8[:, gi * 4096 + k * 1024: gi * 4096 + (k + 1) * 1024]
                    W(e, "t", wt)
                    W(e, "v", w1_)
                    e.scalar_tensor_tensor(ybuf2[su % 2][:, :],
                                           xnsl.bitcast(BF16),
                                           Dsb[:, gi:gi + 1], ypb[:, :],
                                           OP.mult, OP.add).then_inc(SM["v"], 1)
                emit("v", f)
                v_ybuf[su] = CNT["v"]

        def emit_ytr(su):
            k, gi = su // 8, su % 8
            ytr_t = {}
            ytr_a = {}
            for tsub in range(4):
                def f(e, su=su, tsub=tsub, wv=v_ybuf[su],
                      wa=(a_yev.get(su - 1, 0) if tsub < 2 else ytr_a[tsub - 2])):
                    W(e, "v", wv)
                    W(e, "a", wa)
                    e.transpose(ptr[tsub % 2][:, 0:128],
                                ybuf2[su % 2][:, tsub * 128:(tsub + 1) * 128],
                                idn[:, :]).then_inc(SM["t"], 1)
                emit("t", f)
                ytr_t[tsub] = res("t", 1)

                s = Ser("a")
                w1_ = s.op()

                def f(e, su=su, k=k, gi=gi, tsub=tsub, wt=ytr_t[tsub], w1_=w1_,
                      wsi=(si_y.get(k - 1, 0))):
                    W(e, "t", wt)
                    if wsi:
                        W(e, "si", wsi)
                    W(e, "y", YPC * k)
                    W(e, "a", w1_)
                    e.activation(yst[tsub][:, gi * 128:(gi + 1) * 128],
                                 ptr[tsub % 2][:, 0:128],
                                 AF.Identity).then_inc(SM["a"], 1)
                emit("a", f)
                ytr_a[tsub] = CNT["a"]
            t_ytr[su] = ytr_t[3]
            a_yev[su] = ytr_a[3]

            if gi == 7:
                def f(e, k=k, wa=a_yev[su]):
                    W(e, "a", wa)
                    for tsub in range(4):
                        e.dma_start(out=ydr[k * TS + tsub * 128: k * TS + (tsub + 1) * 128, :],
                                    in_=yst[tsub][:, :]).then_inc(SM["y"], 16)
                        if taps:
                            e.dma_start(out=o_ydr[k * TS + tsub * 128: k * TS + (tsub + 1) * 128, :],
                                        in_=yst[tsub][:, :]).then_inc(SM["y"], 16)
                    e.sem_inc(SM["si"], 1)
                emit("sy", f)
                y_ms[k] = res("y", YPC)
                si_y[k] = res("si", 1)

        ccv_ms = {}
        ccv_gc = {}

        def emit_cc(k):
            def f(e, k=k, wy=y_ms[k], wsi=si_y[k]):
                W(e, "si", wsi)
                W(e, "y", wy)
                e.collective_compute(
                    "AllGather", OP.bypass,
                    replica_groups=[[0, 1], [2, 3], [4, 5], [6, 7]],
                    ins=[ydr[k * 512:(k + 1) * 512, :].opt()],
                    outs=[ccd[k * 1024:(k + 1) * 1024, :].opt()],
                ).then_inc(SM["ccv"], 1)
                e.sem_inc(SM["gc"], 1)
            emit("g", f)
            ccv_ms[k] = res("ccv", 1)
            ccv_gc[k] = res("gc", 1)

        for step in range(NHC):
            if step >= 2:
                emit_ytr(2 * step - 4)
                emit_ytr(2 * step - 3)
                if (2 * step - 3) % 8 == 7:
                    emit_cc((2 * step - 3) // 8)
            emit_ffn_strips(step,
                            scan_sus=((2 * step - 2, 2 * step - 1)
                                      if step >= 1 else None))
            emit_ffn_m2(step)
            emit_scan_head(2 * step)
            emit_scan_head(2 * step + 1)
        emit_ytr(2 * NHC - 4)
        emit_ytr(2 * NHC - 3)
        for c in range(4):
            emit_scan_cluster_a(2 * NHC - 2, c)
            emit_scan_cluster(2 * NHC - 2, c)
        for c in range(4):
            emit_scan_cluster_a(2 * NHC - 1, c)
            emit_scan_cluster(2 * NHC - 1, c)
        emit_ytr(2 * NHC - 2)
        emit_ytr(2 * NHC - 1)
        emit_cc(3)

        ACC_TOTAL = CNT["acc"]
        ACC2_TOTAL = CNT["acc2"]
        ccd_flat = ccd.bitcast(U8).rearrange("a b -> (a b)")
        CB = MYCH * 2
        for i in range(2):
            def f(e, i=i, wca=(CNT["cca0"] - 0 if i else 0), wgc=ccv_gc[3]):
                rbC = persist_off(e, "syc4", 2048 * CB, 0)
                base = i * 1024 * CB
                win = ccd_flat[base: base + 2048 * CB + 1024 * CB]
                W(e, "gc", wgc)
                W(e, "ccv", 4)
                if i:
                    W(e, "cca0", wca)
                e.dma_start(
                    out=ccbig[:, :].rearrange("p (s f) -> p s f", s=8),
                    in_=win[ds(rbC, 1024 * CB)]
                        .rearrange("(s p f) -> p s f", s=8, f=CB).bitcast(BF16),
                ).then_inc(SM["ccb"], 16)
                e.sem_inc(SM["si"], 1)
            emit("sy", f)
            ccb_ms = res("ccb", 16)
            si_ccb = res("si", 1)

            for sseg in range(8):
                r, tb = sseg // 4, sseg % 4

                def f(e, i=i, r=r, tb=tb, sseg=sseg, wld=ccb_ms, wsi=si_ccb):
                    W(e, "si", wsi)
                    W(e, "ccb", wld)
                    W(e, "acc", ACC_TOTAL)
                    W(e, "acc2", ACC2_TOTAL)
                    W(e, "xasm", XASM)
                    e.dma_start(
                        out=out[i * 512 + tb * 128: i * 512 + (tb + 1) * 128,
                                r * 1024:(r + 1) * 1024],
                        in_=ccbig[:, sseg * MYCH:(sseg + 1) * MYCH],
                        accum_op=OP.add).then_inc(SM["cca0"], 16)
                emit("g", f)
                res("cca0", 16)

        if taps:
            def f(e, wv=CNT["v"], wa=CNT["a"], wg=GEB):
                W(e, "v", wv)
                W(e, "a", wa)
                W(e, "geB", wg)
                e.dma_start(out=o_xn, in_=xnfm[:, :]).then_inc(SM["tap"], 16)
                e.dma_start(out=o_cols, in_=cols[:, :]).then_inc(SM["tap"], 16)
                e.dma_start(out=o_carry, in_=carry[:, :]).then_inc(SM["tap"], 16)
                e.dma_start(out=o_dtx, in_=dtx[:, :]).then_inc(SM["tap"], 16)
                e.sem_inc(SM["si"], 1)
            emit("sy", f)
            TAPS = res("tap", 64)
            SI_TAP = res("si", 1)

        def f(e, c0=CNT["cca0"]):
            W(e, "acc", ACC_TOTAL)
            W(e, "acc2", ACC2_TOTAL)
            W(e, "cca0", c0)
            W(e, "xasm", XASM)
            if taps:
                W(e, "si", SI_TAP)
                W(e, "tap", TAPS)
        emit("g", f)

        @block.sync
        def _(e):
            for fn in SEQ["sy"]:
                fn(e)

        @block.vector
        def _(e):
            for fn in SEQ["v"]:
                fn(e)

        @block.scalar
        def _(e):
            for fn in SEQ["a"]:
                fn(e)

        @block.tensor
        def _(e):
            for fn in SEQ["t"]:
                fn(e)

        @block.gpsimd
        def _(e):
            for fn in SEQ["g"]:
                fn(e)

    return nc


def make_in_maps(inputs):
    x = np.asarray(inputs["L0_attn_x"], dtype=np.float32)
    W_x = np.asarray(inputs["L0_attn_W_x"], dtype=np.float32)
    W_dt = np.asarray(inputs["L0_attn_W_dt"], dtype=np.float32)
    b_dt = np.asarray(inputs["L0_attn_b_dt"], dtype=np.float32)
    A_log = np.asarray(inputs["L0_attn_A_log"], dtype=np.float32)
    Dv = np.asarray(inputs["L0_attn_D"], dtype=np.float32)
    w_router = np.asarray(inputs["L0_ffn_w_router"], dtype=np.float32)
    w_fc = np.asarray(inputs["L0_ffn_w_fc"], dtype=np.float32)
    w_proj = np.asarray(inputs["L0_ffn_w_proj"], dtype=np.float32)

    wdtb_full = np.concatenate([W_dt.T, b_dt[None, :]], axis=0)
    w2t = np.ascontiguousarray(w_proj.transpose(0, 2, 1))
    ident = np.eye(128, dtype=BF)
    w1t_t = w_fc.transpose(0, 2, 1)
    perm = {0: np.arange(2048),
            1: np.concatenate([np.arange(1024, 2048), np.arange(0, 1024)])}
    wxt_h = {hh: np.ascontiguousarray(W_x.T[perm[hh]]) for hh in (0, 1)}
    wrt_h = {hh: np.ascontiguousarray(w_router[:, perm[hh]]) for hh in (0, 1)}
    w1t_h = {hh: np.ascontiguousarray(w1t_t[:, perm[hh], :]) for hh in (0, 1)}

    in_maps = []
    for c in range(8):
        b, h = c // 2, c % 2
        ch = slice(h * 1024, (h + 1) * 1024)
        in_maps.append({
            "xb": np.ascontiguousarray(x[b]),
            "xbp": np.ascontiguousarray(x[b][:, perm[h]]),
            "wxt": wxt_h[h],
            "wdtb": np.ascontiguousarray(wdtb_full[:, ch]),
            "alog": np.ascontiguousarray(
                A_log[ch].reshape(8, 128, NST).transpose(1, 0, 2).reshape(128, 8 * NST)),
            "dre": np.ascontiguousarray(Dv[ch].reshape(8, 128).T),
            "wrt": wrt_h[h],
            "w1t": w1t_h[h],
            "w2t": w2t,
            "identd": ident,
        })
    return in_maps


def kernel(**inputs):
    nc = build_nc()
    in_maps = make_in_maps(inputs)
    res_ = run_bass_kernel_spmd(nc, in_maps, core_ids=list(range(8)))
    outp = np.empty((4, 2048, 2048), dtype=np.float32)
    for c in range(8):
        b, h = c // 2, c % 2
        outp[b, h * 1024:(h + 1) * 1024, :] = res_.results[c]["out"]
    return outp


# version 26
# speedup vs baseline: 1.0513x; 1.0126x over previous
import numpy as np
import ml_dtypes
import concourse.bass as bass
import concourse.mybir as mybir
from concourse.bass_utils import run_bass_kernel_spmd
from concourse.expressions import RuntimeValue
from contextlib import ExitStack

F32 = mybir.dt.float32
BF16 = mybir.dt.bfloat16
U8 = mybir.dt.uint8
AF = mybir.ActivationFunctionType
OP = mybir.AluOpType
AX = mybir.AxisListType
ds = bass.ds
BF = ml_dtypes.bfloat16

TOK, DM = 2048, 2048
MYTOK, MYCH = 1024, 1024
NST = 16
TS, NCHUNK, NG = 512, 4, 8
HC, NHC = 1024, 16
NTILE = 16
EPS = 1e-5

ENG_SEMS = ("v", "a", "t", "gc")
DMA_SEMS = ("const", "smallw", "xasm", "xt0", "xt1", "gcol", "geB", "bmcd",
            "bc", "y", "w1a", "w1b", "w2", "acc", "acc2", "ccv", "ccl", "cca0", "cca1",
            "tap", "si", "xnw", "xnr", "ccb")
ENGS = ("sy", "v", "a", "t", "g")


def build_nc(taps=False):
    nc = bass.Bass("TRN2", target_bir_lowering=False)

    xb = nc.dram_tensor("xb", [TOK, DM], F32, kind="ExternalInput").ap()
    xbp = nc.dram_tensor("xbp", [TOK, DM], F32, kind="ExternalInput").ap()
    wxt = nc.dram_tensor("wxt", [DM, 96], F32, kind="ExternalInput").ap()
    wdtb = nc.dram_tensor("wdtb", [65, MYCH], F32, kind="ExternalInput").ap()
    alog = nc.dram_tensor("alog", [128, NG * NST], F32, kind="ExternalInput").ap()
    dre = nc.dram_tensor("dre", [128, NG], F32, kind="ExternalInput").ap()
    wrt = nc.dram_tensor("wrt", [2, DM], F32, kind="ExternalInput").ap()
    w1t = nc.dram_tensor("w1t", [2, DM, 8192], F32, kind="ExternalInput").ap()
    w2t = nc.dram_tensor("w2t", [2, 8192, DM], F32, kind="ExternalInput").ap()
    identd = nc.dram_tensor("identd", [128, 128], BF16, kind="ExternalInput").ap()
    out = nc.dram_tensor("out", [MYTOK, DM], F32, kind="ExternalOutput").ap()
    if taps:
        o_xn = nc.dram_tensor("o_xn", [128, 16 * TOK], BF16, kind="ExternalOutput").ap()
        o_cols = nc.dram_tensor("o_cols", [128, 256], F32, kind="ExternalOutput").ap()
        o_carry = nc.dram_tensor("o_carry", [128, 128], F32, kind="ExternalOutput").ap()
        o_dtx = nc.dram_tensor("o_dtx", [65, TOK], BF16, kind="ExternalOutput").ap()
        o_ydr = nc.dram_tensor("o_ydr", [TOK, MYCH], BF16, kind="ExternalOutput").ap()
        o_geB = nc.dram_tensor("o_geB", [128, 2 * MYTOK], BF16, kind="ExternalOutput").ap()

    ydr_t = nc.dram_tensor("ydr", [TOK, MYCH], BF16)
    ccd_t = nc.dram_tensor("ccd", [2 * TOK, MYCH], BF16)
    ydr = ydr_t.ap()
    ccd = ccd_t.ap()
    gdr2 = nc.dram_tensor("gdr2", [128, 32], F32).ap()
    xb_flat = xb.bitcast(U8).rearrange("a b -> (a b)")
    bmcd = nc.dram_tensor("bmcd", [32, TOK], BF16).ap()
    xnd = nc.dram_tensor("xnd", [128, 16 * TOK * 2], U8).ap()

    SEQ = {k: [] for k in ENGS}
    CNT = {k: 0 for k in ENG_SEMS + DMA_SEMS}

    def emit(eng, fn):
        SEQ[eng].append(fn)

    def res(key, n):
        CNT[key] += n
        return CNT[key]

    RT = {}

    def get_base(e, key, mult):
        k2 = (key, mult)
        if k2 not in RT:
            pid = nc.partition_id([e.engine])
            r = e.alloc_register(f"hb_{key}_{mult}")
            e.reg_alu(r, pid, 2, OP.mod)
            e.reg_alu(r, r, mult, OP.mult)
            RT[k2] = r
        return RT[k2]

    def off_reg(e, key, idx=0):
        k2 = ("off", key, idx)
        if k2 not in RT:
            RT[k2] = e.alloc_register(f"off_{key}_{idx}")
        return RT[k2]

    def persist_off(e, key, mult, const):
        k2 = ("p", key, mult, const)
        if k2 not in RT:
            rb = get_base(e, key, mult)
            r = e.alloc_register(f"po_{key}_{const}")
            e.reg_alu(r, rb, const, OP.add)
            RT[k2] = RuntimeValue(r, min_val=const, max_val=mult + const)
        return RT[k2]

    def rv(reg, lo, hi):
        return RuntimeValue(reg, min_val=lo, max_val=hi)

    with ExitStack() as ctx:
        def sb(nm, shp, dt):
            return ctx.enter_context(nc.sbuf_tensor(nm, shp, dt))

        def psum(nm, shp, dt):
            return ctx.enter_context(nc.psum_tensor(nm, shp, dt))

        xnfm = sb("xnfm", [128, 16 * TOK], BF16)
        xnu8 = xnfm[:, :].bitcast(U8)
        dtx = sb("dtx", [65, TOK], BF16)
        wdtbs = sb("wdtbs", [65, MYCH], BF16)
        Asb = sb("Asb", [128, NG * NST], F32)
        Dsb = sb("Dsb", [128, NG], F32)
        idn = sb("idn", [128, 128], BF16)
        carry = sb("carry", [128, 128], F32)
        gtok = sb("gtok", [128, 16], F32)
        stage2 = sb("stage2", [128, DM], BF16)
        cols = sb("cols", [128, 16 * 16], F32)

        (C_S1, C_S2, C_SXW0, C_SXW1, C_M, C_VPS, C_RSTD, C_NMR, C_D, C_SIG,
         C_GE, C_G0, C_G1, C_T0, C_T1, C_T2) = range(16)

        def col(idx, i):
            return cols[:, idx * 16 + i: idx * 16 + i + 1]

        def colsl(idx):
            return cols[:, idx * 16: idx * 16 + 16]

        UB = 134144
        union = sb("union", [128, UB], U8)

        def carve(off_bytes, nelem, dt):
            esz = {F32: 4, BF16: 2}[dt]
            return union[:, off_bytes: off_bytes + nelem * esz].bitcast(dt)

        class Cursor:
            def __init__(self):
                self.o = 0

            def take(self, nelem, dt):
                esz = {F32: 4, BF16: 2}[dt]
                ap = carve(self.o, nelem, dt)
                self.o += nelem * esz
                self.o = (self.o + 15) & ~15
                return ap

        cu = Cursor()
        w1pp = [cu.take(16 * 128, BF16), cu.take(16 * 128, BF16)]
        w2c = cu.take(8 * DM, BF16)
        aslot = cu.take(8 * MYTOK, BF16)
        stg = [cu.take(DM, BF16), None]
        fbuf = [cu.take(512, BF16) for _ in range(3)]
        sqb = [cu.take(512, BF16)]
        BmBb = cu.take(NST * TS, BF16)
        CmBb = cu.take(NST * TS, BF16)
        dtg = [cu.take(TS, F32), cu.take(TS, F32)]
        ug = [cu.take(TS, BF16), cu.take(TS, BF16)]
        dab = [cu.take(TS, BF16) for _ in range(4)]
        dbxb = [cu.take(TS, BF16) for _ in range(4)]
        hball = cu.take(4 * TS, BF16)
        hb = [hball[:, i * TS:(i + 1) * TS] for i in range(4)]
        hCall = cu.take(4 * TS, BF16)
        hC = [hCall[:, i * TS:(i + 1) * TS] for i in range(4)]
        ypb = cu.take(TS, BF16)
        ybuf2 = [cu.take(TS, BF16), cu.take(TS, BF16)]
        yst = [cu.take(MYCH, BF16) for _ in range(4)]
        t1s = cu.take(TS, F32)
        assert cu.o <= UB, cu.o

        cu = Cursor()
        xt = [cu.take(DM, F32), cu.take(DM, F32)]
        wrB = [cu.take(DM, F32), cu.take(DM, F32)]
        sqscr = cu.take(DM, BF16)
        sqg = cu.take(DM, BF16)
        bmcm = cu.take(TOK, F32)
        xnt = [cu.take(DM, BF16), cu.take(DM, BF16)]
        xasm = cu.take(DM, F32)
        wxts = cu.take(16 * 96, BF16)
        swc = cu.take(2, F32)
        assert cu.o <= UB, cu.o

        ccbig = carve(72 * 1024, 8 * MYCH, BF16)

        stg[1] = stage2[:, :]
        pm1 = [psum("pm1a", [128, 512], F32), psum("pm1b", [128, 512], F32),
               psum("pm1c", [128, 512], F32)]
        pm2 = [psum("pm2a", [128, 512], F32), psum("pm2b", [128, 512], F32)]
        pdt = psum("pdt", [128, 512], F32)
        ptr = [psum("ptra", [128, 512], BF16), psum("ptrb", [128, 512], BF16)]

        SM = {k: ctx.enter_context(nc.semaphore(f"s_{k}"))
              for k in ENG_SEMS + DMA_SEMS}
        block = ctx.enter_context(nc.Block())

        def W(e, key, val):
            if val:
                e.wait_ge(SM[key], val)

        class Ser:

            def __init__(self, key):
                self.key = key
                self.base = CNT[key]
                self.n = 0

            def op(self):
                w = self.base + self.n
                self.n += 1
                res(self.key, 1)
                return w

        def f(e):
            e.dma_start(out=idn[:, :], in_=identd).then_inc(SM["const"], 16)
            e.dma_start(out=Asb[:, :], in_=alog).then_inc(SM["const"], 16)
            e.dma_start(out=Dsb[:, :], in_=dre).then_inc(SM["const"], 16)
            e.dma_start(out=wrB[0][:, :].rearrange("p (o f) -> p o f", o=1),
                        in_=wrt[0:1, :].partition_broadcast(128)).then_inc(SM["const"], 16)
            e.dma_start(out=wrB[1][:, :].rearrange("p (o f) -> p o f", o=1),
                        in_=wrt[1:2, :].partition_broadcast(128)).then_inc(SM["const"], 16)
            e.sem_inc(SM["si"], 1)
        emit("sy", f)
        CONSTS = res("const", 5 * 16)
        SI_CONST = res("si", 1)

        def f(e):
            e.dma_start(out=wxts[:, :].rearrange("p (a m) -> p a m", a=16),
                        in_=wxt.rearrange("(a p) m -> p a m", p=128)).then_inc(SM["smallw"], 16)
            e.dma_start(out=wdtbs[:, :], in_=wdtb).then_inc(SM["smallw"], 16)
            e.sem_inc(SM["gc"], 1)
        emit("g", f)
        SMALLW = res("smallw", 2 * 16)
        GC_SMALLW = res("gc", 1)

        for tt in range(8):
            w0 = CNT["xasm"]

            def f(e, tt=tt, w0=w0):
                rbX = persist_off(e, "syx", 1024 * DM * 4, 0)
                base = tt * 128 * DM * 4
                win = xb_flat[base: base + 1024 * DM * 4 + 128 * DM * 4]
                W(e, "xasm", w0)
                e.dma_start(out=xasm[:, :],
                            in_=win[ds(rbX, 128 * DM * 4)]
                                .rearrange("(p f) -> p f", f=DM * 4).bitcast(F32)
                            ).then_inc(SM["xasm"], 16)
                W(e, "xasm", w0 + 16)
                e.dma_start(out=out[tt * 128:(tt + 1) * 128, :],
                            in_=xasm[:, :]).then_inc(SM["xasm"], 16)
            emit("sy", f)
            res("xasm", 32)
        XASM = CNT["xasm"]

        sa = Ser("a")
        wA0 = sa.op()

        def f(e, w=wA0):
            W(e, "si", SI_CONST)
            W(e, "const", CONSTS)
            W(e, "a", w)
            e.activation(Asb[:, :], Asb[:, :], AF.Exp).then_inc(SM["a"], 1)
        emit("a", f)
        a_Aexp = CNT["a"]

        sv = Ser("v")
        wV0 = sv.op()

        def f(e, w=wV0, wa=a_Aexp):
            W(e, "a", wa)
            W(e, "v", w)
            e.tensor_scalar(Asb[:, :], Asb[:, :], -1.0, None, OP.mult).then_inc(SM["v"], 1)
        emit("v", f)
        v_Aneg = CNT["v"]

        vWAR = [0] * NTILE
        aWAR = [0] * NTILE
        gWAR = [0] * NTILE
        t_tr = {}
        a_ev = {}

        for i in range(NTILE):
            xk = "xt%d" % (i % 2)

            def f(e, i=i, xk=xk, wv=(vWAR[i - 2] if i >= 2 else 0),
                  wa=(aWAR[i - 2] if i >= 2 else 0),
                  wg=0):
                W(e, "v", wv)
                W(e, "a", wa)
                e.dma_start(out=xt[i % 2][:, :],
                            in_=xbp[i * 128:(i + 1) * 128, :]).then_inc(SM[xk], 16)
                e.sem_inc(SM["si"], 1)
            emit("sy", f)
            xt_ms = res(xk, 16)
            si_xt = res("si", 1)

            s = Ser("v")
            ws = [s.op() for _ in range(9)]

            def f(e, i=i, xk=xk, wx=xt_ms, ws=ws, wsi=si_xt):
                W(e, "si", wsi)
                W(e, xk, wx)
                W(e, "const", CONSTS)
                x = xt[i % 2][:, :]
                W(e, "v", ws[0])
                e.tensor_reduce(col(C_S1, i), x, AX.X, OP.add).then_inc(SM["v"], 1)
                W(e, "v", ws[1])
                e.scalar_tensor_tensor(sqscr[:, :], x, 1.0, x, OP.mult, OP.mult,
                                       accum_out=col(C_S2, i)).then_inc(SM["v"], 1)
                W(e, "v", ws[2])
                e.scalar_tensor_tensor(sqscr[:, :], x, 1.0, wrB[0][:, :], OP.mult, OP.mult,
                                       accum_out=col(C_SXW0, i)).then_inc(SM["v"], 1)
                W(e, "v", ws[3])
                e.scalar_tensor_tensor(sqscr[:, :], x, 1.0, wrB[1][:, :], OP.mult, OP.mult,
                                       accum_out=col(C_SXW1, i)).then_inc(SM["v"], 1)
                W(e, "v", ws[4])
                e.tensor_scalar(col(C_M, i), col(C_S1, i), 1.0 / DM, None, OP.mult).then_inc(SM["v"], 1)
                W(e, "v", ws[5])
                e.tensor_scalar(col(C_T0, i), col(C_S2, i), 1.0 / DM, None, OP.mult).then_inc(SM["v"], 1)
                W(e, "v", ws[6])
                e.tensor_tensor(col(C_T1, i), col(C_M, i), col(C_M, i), OP.mult).then_inc(SM["v"], 1)
                W(e, "v", ws[7])
                e.tensor_tensor(col(C_T2, i), col(C_T0, i), col(C_T1, i), OP.subtract).then_inc(SM["v"], 1)
                W(e, "v", ws[8])
                e.tensor_scalar(col(C_VPS, i), col(C_T2, i), EPS, None, OP.add).then_inc(SM["v"], 1)
            emit("v", f)
            v_stats = CNT["v"]

            s = Ser("a")
            ws = [s.op() for _ in range(2)]

            def f(e, i=i, wv=v_stats, ws=ws):
                W(e, "v", wv)
                W(e, "a", ws[0])
                e.activation(col(C_T0, i), col(C_VPS, i), AF.Ln).then_inc(SM["a"], 1)
                W(e, "a", ws[1])
                e.activation(col(C_RSTD, i), col(C_T0, i), AF.Exp, scale=-0.5).then_inc(SM["a"], 1)
            emit("a", f)
            a_rstd = CNT["a"]

            s = Ser("v")
            ws = [s.op() for _ in range(2)]

            def f(e, i=i, wa=a_rstd, ws=ws):
                W(e, "a", wa)
                W(e, "v", ws[0])
                e.tensor_tensor(col(C_T1, i), col(C_M, i), col(C_RSTD, i), OP.mult).then_inc(SM["v"], 1)
                W(e, "v", ws[1])
                e.tensor_scalar(col(C_NMR, i), col(C_T1, i), -1.0, None, OP.mult).then_inc(SM["v"], 1)
            emit("v", f)
            vWAR[i] = CNT["v"]

            s = Ser("a")
            w1_ = s.op()

            def f(e, i=i, wv=vWAR[i], w1_=w1_):
                W(e, "v", wv)
                W(e, "a", w1_)
                e.activation(xnt[i % 2][:, :], xt[i % 2][:, :], AF.Identity,
                             bias=col(C_NMR, i), scale=col(C_RSTD, i)).then_inc(SM["a"], 1)
            emit("a", f)
            aWAR[i] = CNT["a"]

            for j in range(16):
                li = i * 16 + j

                def f(e, i=i, j=j, li=li, wa=max(aWAR[i], a_ev.get(li - 2, 0))):
                    if li == 0:
                        W(e, "const", CONSTS)
                    W(e, "a", wa)
                    e.transpose(ptr[j % 2][:, 0:128],
                                xnt[i % 2][:, j * 128:(j + 1) * 128],
                                idn[:, :]).then_inc(SM["t"], 1)
                emit("t", f)
                t_tr[li] = res("t", 1)

                s = Ser("a")
                w1_ = s.op()

                def f(e, i=i, j=j, wt=t_tr[li], w1_=w1_):
                    W(e, "t", wt)
                    W(e, "a", w1_)
                    e.activation(xnfm[:, j * TOK + i * 128: j * TOK + (i + 1) * 128],
                                 ptr[j % 2][:, 0:128], AF.Identity).then_inc(SM["a"], 1)
                emit("a", f)
                a_ev[li] = CNT["a"]

        xn_done_a = CNT["a"]

        s = Ser("v")
        ws = [s.op() for _ in range(10)]

        def f(e, ws=ws):
            ops = [
                lambda: e.tensor_reduce(swc[:, 0:1], wrB[0][:, :], AX.X, OP.add),
                lambda: e.tensor_reduce(swc[:, 1:2], wrB[1][:, :], AX.X, OP.add),
                lambda: e.tensor_tensor(colsl(C_T0), colsl(C_M),
                                        swc[:, 0:1].broadcast_to([128, 16]), OP.mult),
                lambda: e.tensor_tensor(colsl(C_T1), colsl(C_SXW0), colsl(C_T0), OP.subtract),
                lambda: e.tensor_tensor(colsl(C_G0), colsl(C_T1), colsl(C_RSTD), OP.mult),
                lambda: e.tensor_tensor(colsl(C_T0), colsl(C_M),
                                        swc[:, 1:2].broadcast_to([128, 16]), OP.mult),
                lambda: e.tensor_tensor(colsl(C_T1), colsl(C_SXW1), colsl(C_T0), OP.subtract),
                lambda: e.tensor_tensor(colsl(C_G1), colsl(C_T1), colsl(C_RSTD), OP.mult),
                lambda: e.tensor_tensor(colsl(C_D), colsl(C_G0), colsl(C_G1), OP.subtract),
            ]
            for w, op_ in zip(ws, ops):
                W(e, "v", w)
                op_().then_inc(SM["v"], 1)
        emit("v", f)
        v_dcol = CNT["v"] - 1
        res("v", 0)
        CNT["v"] -= 1
        v_dcol = CNT["v"]

        s = Ser("a")
        w1_ = s.op()

        def f(e, wv=v_dcol, w1_=w1_):
            W(e, "v", wv)
            W(e, "a", w1_)
            e.activation(colsl(C_SIG), colsl(C_D), AF.Sigmoid).then_inc(SM["a"], 1)
        emit("a", f)
        a_sig = CNT["a"]

        s = Ser("v")
        ws = [s.op() for _ in range(5)]

        def f(e, wa=a_sig, ws=ws):
            W(e, "a", wa)
            ops = [
                lambda: e.tensor_scalar(colsl(C_GE), colsl(C_D), 0.0, None, OP.is_ge),
                lambda: e.tensor_tensor(colsl(C_G0), colsl(C_SIG), colsl(C_GE), OP.mult),
                lambda: e.tensor_scalar(colsl(C_T0), colsl(C_SIG), -1.0, 1.0, OP.mult, OP.add),
                lambda: e.tensor_scalar(colsl(C_T1), colsl(C_GE), -1.0, 1.0, OP.mult, OP.add),
                lambda: e.tensor_tensor(colsl(C_G1), colsl(C_T0), colsl(C_T1), OP.mult),
            ]
            for w, op_ in zip(ws, ops):
                W(e, "v", w)
                op_().then_inc(SM["v"], 1)
        emit("v", f)
        v_g = CNT["v"]

        def f(e, w=v_g):
            W(e, "v", w)
            for hh in range(2):
                e.dma_start(out=gdr2[:, hh * 16: hh * 16 + 8],
                            in_=cols[:, C_G0 * 16 + hh * 8: C_G0 * 16 + hh * 8 + 8]
                            ).then_inc(SM["gcol"], 16)
                e.dma_start(out=gdr2[:, hh * 16 + 8: hh * 16 + 16],
                            in_=cols[:, C_G1 * 16 + hh * 8: C_G1 * 16 + hh * 8 + 8]
                            ).then_inc(SM["gcol"], 16)
            e.sem_inc(SM["si"], 1)
        emit("sy", f)
        GCOL = res("gcol", 64)
        SI_GCOL = res("si", 1)

        def f(e):
            rbx = persist_off(e, "ggx", 16, 0)
            W(e, "si", SI_GCOL)
            W(e, "gcol", GCOL)
            e.dma_start(out=gtok[:, 0:16],
                        in_=gdr2[:, ds(rbx, 16)]).then_inc(SM["geB"], 16)
            e.sem_inc(SM["si"], 1)
        emit("sy", f)
        GEB = res("geB", 16)
        SI_GEB = res("si", 1)

        xp_a = {}
        xp_v = {}
        t_xp = {}
        for tc in range(4):
            def f(e, tc=tc, wvv=xp_v.get(tc - 2, 0),
                  wa=max(xn_done_a, xp_a.get(tc - 2, 0))):
                W(e, "gc", GC_SMALLW)
                W(e, "smallw", SMALLW)
                W(e, "v", wvv)
                W(e, "a", wa)
                p = pm1[tc % 2]
                for dc in range(16):
                    mm = e.matmul(p[0:96, :], wxts[:, dc * 96:(dc + 1) * 96],
                                  xnfm[:, dc * TOK + tc * 512: dc * TOK + (tc + 1) * 512],
                                  start=(dc == 0), stop=(dc == 15))
                mm.then_inc(SM["t"], 1)
            emit("t", f)
            t_xp[tc] = res("t", 1)

            s = Ser("a")
            w1_ = s.op()

            def f(e, tc=tc, wt=t_xp[tc], w1_=w1_):
                W(e, "t", wt)
                W(e, "a", w1_)
                e.activation(dtx[0:64, tc * 512:(tc + 1) * 512],
                             pm1[tc % 2][0:64, :], AF.Identity).then_inc(SM["a"], 1)
            emit("a", f)
            xp_a[tc] = CNT["a"]

            s = Ser("v")
            w1_ = s.op()

            def f(e, tc=tc, wt=t_xp[tc], w1_=w1_):
                W(e, "t", wt)
                W(e, "v", w1_)
                e.tensor_copy(bmcm[64:96, tc * 512:(tc + 1) * 512],
                              pm1[tc % 2][64:96, :]).then_inc(SM["v"], 1)
            emit("v", f)
            xp_v[tc] = CNT["v"]

        s = Ser("v")
        w1_ = s.op()

        def f(e, w=xp_v[3], w1_=w1_):
            W(e, "v", max(w, w1_))
            e.memset(dtx[64:65, :], 1.0).then_inc(SM["v"], 1)
        emit("v", f)
        v_ones = CNT["v"]

        def f(e, w=v_ones):
            W(e, "v", w)
            e.dma_start(out=bmcd, in_=bmcm[64:96, :]).then_inc(SM["bmcd"], 16)
            e.sem_inc(SM["gc"], 1)
        emit("g", f)
        BMCD = res("bmcd", 16)
        GC_BMCD = res("gc", 1)

        def f(e, wa=xn_done_a):
            W(e, "a", wa)
            e.dma_start(out=xnd, in_=xnu8).then_inc(SM["xnw"], 16)
            e.sem_inc(SM["si"], 1)
        emit("sy", f)
        XNW = res("xnw", 16)
        SI_XNW = res("si", 1)

        def f(e, wt=t_xp[3]):
            rbB = persist_off(e, "sy", 2048, 0)
            W(e, "si", SI_XNW)
            W(e, "xnw", XNW)
            W(e, "t", wt)
            e.dma_start(
                out=xnu8[:, 8 * 4096: 16 * 4096].rearrange("p (a f) -> p a f", a=16),
                in_=xnd.rearrange("p (a f) -> p a f", a=16)[:, :, ds(rbB, 2048)],
            ).then_inc(SM["xnr"], 16)
            e.sem_inc(SM["si"], 1)
        emit("sy", f)
        XNR = res("xnr", 16)
        SI_XNR = res("si", 1)

        EARLY = dict(v=v_ones, a=xp_a[3], t=t_xp[3], gc=GC_BMCD,
                     si=max(SI_XNR, SI_GEB))
        EARLY_DMA = dict(const=CONSTS, smallw=SMALLW, xasm=XASM, bmcd=BMCD,
                         gcol=GCOL, geB=GEB, xnr=XNR)

        for eng in ENGS:
            def f(e, eng=eng):
                for k2, val in EARLY.items():
                    W(e, k2, val)
                for k2, val in EARLY_DMA.items():
                    W(e, k2, val)
            emit(eng, f)

        ffn_acc = {}
        a_ev2 = {}
        v_stt = {}
        v_am = {}
        a_sq = {}
        a_ev1 = {}
        t_m1 = {}
        t_m2 = {}
        t_m2l = {}
        v_scan = {}
        g_hC = {}
        g_dbx = {}
        a_dA = {}
        v_dtg = {}
        a_sp = {}
        g_tree = {}
        g_ypb = {}
        v_ybuf = {}
        t_ytr = {}
        a_yev = {}
        y_ms = {}
        bc_ms = {}
        w1_ms = {}
        w2_ms = {}
        w1_gc = {}
        w2_gc = {}
        acc_gc = {}
        si_bc = {}
        si_y = {}

        NSTRIPS = NHC * 8
        YPC = 128 if taps else 64

        def w1_load(strip):
            wk = "w1%s" % ("ab"[strip % 2])

            def f(e, strip=strip, wk=wk, wt=t_m1.get(strip - 2, 0)):
                W(e, "t", wt)
                hc, sидx = strip // 8, strip % 8
                e_idx = hc // 8
                c0 = (hc % 8) * HC + sидx * 128
                e.dma_start(
                    out=w1pp[strip % 2][:, :].rearrange("p (a f) -> p a f", a=16),
                    in_=w1t[e_idx:e_idx + 1, :, c0:c0 + 128].rearrange(
                        "o (a p) f -> p (o a) f", p=128),
                ).then_inc(SM[wk], 16)
                e.sem_inc(SM["gc"], 1)
            emit("g", f)
            w1_ms[strip] = res(wk, 16)
            w1_gc[strip] = res("gc", 1)

        def w2_load(hc):
            def f(e, hc=hc, wt=t_m2.get(hc - 1, 0)):
                W(e, "t", wt)
                e_idx = hc // 8
                r0 = (hc % 8) * HC
                e.dma_start(
                    out=w2c[:, :].rearrange("p (a f) -> p a f", a=8),
                    in_=w2t[e_idx:e_idx + 1, r0:r0 + HC, :].rearrange(
                        "o (a p) f -> p (o a) f", p=128),
                ).then_inc(SM["w2"], 16)
                e.sem_inc(SM["gc"], 1)
            emit("g", f)
            w2_ms[hc] = res("w2", 16)
            w2_gc[hc] = res("gc", 1)

        w1_load(0)
        w1_load(1)
        w2_load(0)

        def emit_ffn_strips(hc, scan_sus=None):
            e_idx = hc // 8
            for sx in range(8):
                strip = hc * 8 + sx
                wk = "w1%s" % ("ab"[strip % 2])

                t_m1h = {}
                for tt in range(2):
                    q = strip * 2 + tt

                    def f(e, strip=strip, tt=tt, q=q, wk=wk, wg=w1_ms[strip],
                          wgc=w1_gc[strip], wv=a_ev1.get(q - 3, 0)):
                        if tt == 0:
                            W(e, "gc", wgc)
                            W(e, wk, wg)
                        W(e, "a", wv)
                        for dc in range(16):
                            base = 8 * 4096 + dc * 2048 + tt * 1024
                            mm = e.matmul(pm1[q % 3][:, :],
                                          w1pp[strip % 2][:, dc * 128:(dc + 1) * 128],
                                          xnu8[:, base: base + 1024].bitcast(BF16),
                                          start=(dc == 0), stop=(dc == 15))
                        mm.then_inc(SM["t"], 1)
                    emit("t", f)
                    t_m1h[q] = res("t", 1)
                t_m1[strip] = t_m1h[strip * 2 + 1]
                if strip + 2 < NSTRIPS:
                    w1_load(strip + 2)

                if sx >= 1:
                    emit_act_chain(hc, strip - 1)
                if scan_sus is not None:
                    emit_scan_cluster_a(scan_sus[sx // 4], sx % 4)

                for tt in range(2):
                    q = strip * 2 + tt

                    s = Ser("a")
                    w1_ = s.op()

                    def f(e, strip=strip, tt=tt, q=q, wt=t_m1h[q], w1_=w1_):
                        W(e, "t", wt)
                        W(e, "a", w1_)
                        e.activation(fbuf[q % 3][:, :], pm1[q % 3][:, :],
                                     AF.Identity).then_inc(SM["a"], 1)
                    emit("a", f)
                    a_ev1[q] = CNT["a"]

                if sx == 7:
                    emit_act_chain(hc, strip)
                if scan_sus is not None:
                    emit_scan_cluster(scan_sus[sx // 4], sx % 4)

        def emit_act_chain(hc, strip):
            e_idx = hc // 8
            sx = strip % 8
            for tt in range(2):
                q = strip * 2 + tt

                s = Ser("v")
                w1_ = s.op()

                def f(e, q=q, wa=a_ev1[q], w1_=w1_, wsq=a_sq.get(q - 1, 0)):
                    W(e, "a", max(wa, wsq))
                    W(e, "v", w1_)
                    f0 = fbuf[q % 3][:, :]
                    e.scalar_tensor_tensor(sqb[0][:, :], f0, 0.5, f0,
                                           OP.mult, OP.max).then_inc(SM["v"], 1)
                emit("v", f)
                v_stt[q] = CNT["v"]

                s = Ser("a")
                w1_ = s.op()

                def f(e, sx=sx, tt=tt, q=q, wv=v_stt[q], w1_=w1_):
                    W(e, "v", wv)
                    W(e, "a", w1_)
                    e.activation(
                        aslot[:, sx * MYTOK + tt * 512: sx * MYTOK + (tt + 1) * 512],
                        sqb[0][:, :], AF.Square).then_inc(SM["a"], 1)
                emit("a", f)
                a_sq[q] = CNT["a"]

        def emit_ffn_m2(hc):
            a_ready = a_sq[(hc * 8 + 7) * 2 + 1]
            for tt in range(8):
                for dq in range(4):
                    m = (hc * 8 + tt) * 4 + dq

                    def f(e, hc=hc, tt=tt, dq=dq, m=m, wv=a_ready,
                          wg=w2_ms[hc], wgc=w2_gc[hc], wae=a_ev2.get(m - 2, 0)):
                        W(e, "a", wv)
                        W(e, "gc", wgc)
                        W(e, "w2", wg)
                        W(e, "a", wae)
                        p = pm2[m % 2]
                        for sx in range(8):
                            mm = e.matmul(p[:, :],
                                          aslot[:, sx * MYTOK + tt * 128: sx * MYTOK + tt * 128 + 128],
                                          w2c[:, sx * DM + dq * 512: sx * DM + (dq + 1) * 512],
                                          start=(sx == 0), stop=(sx == 7))
                        mm.then_inc(SM["t"], 1)
                    emit("t", f)
                    t_m2l[m] = res("t", 1)

                    s = Ser("a")
                    w1_ = s.op()

                    def f(e, hc=hc, tt=tt, dq=dq, m=m, wt=t_m2l[m], w1_=w1_,
                          wgacc=acc_gc.get(hc * 8 + tt - 2, 0)):
                        L = hc * 8 + tt
                        W(e, "t", wt)
                        if dq == 0 and wgacc:
                            W(e, "gc", wgacc)
                            W(e, "acc" if L % 2 == 0 else "acc2",
                              16 * (L // 2))
                        W(e, "a", w1_)
                        e.activation(stg[L % 2][:, dq * 512:(dq + 1) * 512],
                                     pm2[m % 2][:, :], AF.Identity,
                                     scale=gtok[:, (hc // 8) * 8 + tt:
                                                (hc // 8) * 8 + tt + 1]).then_inc(SM["a"], 1)
                    emit("a", f)
                    a_ev2[m] = CNT["a"]

                L = hc * 8 + tt

                def f(e, hc=hc, tt=tt, L=L, wa=a_ev2[(hc * 8 + tt) * 4 + 3]):
                    W(e, "a", wa)
                    W(e, "xasm", XASM)
                    e.dma_start(out=out[tt * 128:(tt + 1) * 128, :], in_=stg[L % 2][:, :],
                                accum_op=OP.add).then_inc(
                                    SM["acc" if L % 2 == 0 else "acc2"], 16)
                    e.sem_inc(SM["gc"], 1)
                emit("g", f)
                ffn_acc[L] = res("acc" if L % 2 == 0 else "acc2", 16)
                acc_gc[L] = res("gc", 1)

            t_m2[hc] = t_m2l[(hc * 8 + 7) * 4 + 3]
            if hc + 1 < NHC:
                w2_load(hc + 1)

        def emit_tree(su, qq):
            s = Ser("v")
            ws = [s.op() for _ in range(3)]

            def f(e, su=su, qq=qq, ws=ws):
                pv = hCall[:, :].rearrange("p (s t) -> p s t", s=2)
                W(e, "v", ws[0])
                e.tensor_tensor(pv[:, :, 0:TS], pv[:, :, 0:TS],
                                pv[:, :, TS:2 * TS], OP.add).then_inc(SM["v"], 1)
                W(e, "v", ws[1])
                e.tensor_tensor(hC[0][:, :], hC[0][:, :], hC[2][:, :], OP.add).then_inc(SM["v"], 1)
                W(e, "v", ws[2])
                if qq == 0:
                    e.tensor_copy(ypb[:, :], hC[0][:, :]).then_inc(SM["v"], 1)
                else:
                    e.tensor_tensor(ypb[:, :], ypb[:, :], hC[0][:, :],
                                    OP.add).then_inc(SM["v"], 1)
            emit("v", f)

        def emit_scan_head(su):
            k, gi = su // 8, su % 8
            pp = su % 2

            if gi == 0:
                def f(e, k=k,
                      wv=v_ybuf.get(su - 1, 0)):
                    W(e, "bmcd", BMCD)
                    W(e, "v", wv)
                    e.dma_start(out=BmBb[:, :].rearrange("p (n t) -> p n t", n=NST),
                                in_=bmcd[0:16, k * TS:(k + 1) * TS].partition_broadcast(128)
                                ).then_inc(SM["bc"], 16)
                    e.dma_start(out=CmBb[:, :].rearrange("p (n t) -> p n t", n=NST),
                                in_=bmcd[16:32, k * TS:(k + 1) * TS].partition_broadcast(128)
                                ).then_inc(SM["bc"], 16)
                    e.sem_inc(SM["si"], 1)
                emit("sy", f)
                bc_ms[k] = res("bc", 32)
                si_bc[k] = res("si", 1)

            def f(e, su=su, k=k, gi=gi, wfree=v_dtg.get(su - 1, 0)):
                W(e, "v", wfree)
                e.matmul(pdt[:, :], wdtbs[:, gi * 128:(gi + 1) * 128],
                         dtx[:, k * TS:(k + 1) * TS], start=True, stop=True).then_inc(SM["t"], 1)
            emit("t", f)
            t_dt = res("t", 1)

            s = Ser("a")
            ws = [s.op() for _ in range(3)]

            def f(e, su=su, wt=t_dt, wfree=v_dtg.get(su - 1, 0), ws=ws):
                W(e, "t", wt)
                W(e, "v", wfree)
                W(e, "a", ws[0])
                e.activation(t1s[:, :], pdt[:, :], AF.Abs).then_inc(SM["a"], 1)
                W(e, "a", ws[1])
                e.activation(t1s[:, :], t1s[:, :], AF.Exp, scale=-1.0).then_inc(SM["a"], 1)
                W(e, "a", ws[2])
                e.activation(t1s[:, :], t1s[:, :], AF.Ln, bias=1.0).then_inc(SM["a"], 1)
            emit("a", f)
            a_sp[su] = CNT["a"]

            s = Ser("v")
            ws = [s.op() for _ in range(3)]

            def f(e, su=su, k=k, gi=gi, pp=pp, wt=t_dt, wa=a_sp[su],
                  wdA=a_dA.get((su - 2, NST - 1), 0), ws=ws):
                xnsl = xnu8[:, gi * 4096 + k * 1024: gi * 4096 + (k + 1) * 1024]
                W(e, "t", wt)
                W(e, "a", max(wa, wdA))
                W(e, "v", ws[0])
                e.tensor_scalar(dtg[pp][:, :], pdt[:, :], 0.0, None, OP.max).then_inc(SM["v"], 1)
                W(e, "v", ws[1])
                e.tensor_tensor(dtg[pp][:, :], dtg[pp][:, :], t1s[:, :], OP.add).then_inc(SM["v"], 1)
                W(e, "v", ws[2])
                e.tensor_tensor(ug[pp][:, :], dtg[pp][:, :],
                                xnsl.bitcast(BF16),
                                OP.mult).then_inc(SM["v"], 1)
            emit("v", f)
            v_dtg[su] = CNT["v"]

        def emit_scan_cluster_a(su, c):
            gi = su % 8
            for n in range(4 * c, 4 * c + 4):
                tile = gi * NST + n
                wfree_scan = (v_scan.get((su, n - 4), 0) if n >= 4
                              else v_scan.get((su - 1, NST - 4 + n), 0))

                s = Ser("a")
                w1_ = s.op()

                def f(e, su=su, n=n, tile=tile, wv=v_dtg[su],
                      wfree=wfree_scan, wA=v_Aneg, w1_=w1_):
                    W(e, "v", max(wv, wA, wfree))
                    W(e, "a", w1_)
                    e.activation(dab[n % 4][:, :], dtg[su % 2][:, :], AF.Exp,
                                 scale=Asb[:, tile:tile + 1]).then_inc(SM["a"], 1)
                emit("a", f)
                a_dA[(su, n)] = CNT["a"]

        def emit_scan_cluster(su, c):
            k, gi = su // 8, su % 8

            for n in range(4 * c, 4 * c + 4):
                tile = gi * NST + n
                npp = n % 4

                s = Ser("v")
                ws = [s.op() for _ in range(3)]

                def f(e, su=su, k=k, n=n, npp=npp, tile=tile,
                      wa=a_dA[(su, n)], ws=ws):
                    W(e, "si", si_bc[k])
                    W(e, "bc", bc_ms[k])
                    W(e, "a", wa)
                    init = 0.0 if k == 0 else carry[:, tile:tile + 1]
                    W(e, "v", ws[0])
                    e.tensor_tensor(dbxb[npp][:, :], ug[su % 2][:, :],
                                    BmBb[:, n * TS:(n + 1) * TS], OP.mult).then_inc(SM["v"], 1)
                    W(e, "v", ws[1])
                    e.tensor_tensor_scan(hb[npp][:, :], dab[n % 4][:, :], dbxb[npp][:, :],
                                         init, OP.mult, OP.add).then_inc(SM["v"], 1)
                    W(e, "v", ws[2])
                    e.tensor_tensor(hC[n % 4][:, :], hb[npp][:, :],
                                    CmBb[:, n * TS:(n + 1) * TS], OP.mult).then_inc(SM["v"], 1)
                emit("v", f)
                v_scan[(su, n)] = CNT["v"]

            if su < 24:
                t0 = gi * NST + 4 * c

                s = Ser("v")
                w1_ = s.op()

                def f(e, t0=t0, w1_=w1_):
                    W(e, "v", w1_)
                    e.tensor_copy(
                        carry[:, t0:t0 + 4].rearrange("p (s o) -> p s o", o=1),
                        hball[:, :].rearrange("p (s t) -> p s t", s=4)[:, :, TS - 1:TS],
                    ).then_inc(SM["v"], 1)
                emit("v", f)

            emit_tree(su, c)
            if c == 3:
                k, gi = su // 8, su % 8

                s = Ser("v")
                w1_ = s.op()

                def f(e, su=su, k=k, gi=gi, wt=t_ytr.get(su - 2, 0), w1_=w1_):
                    xnsl = xnu8[:, gi * 4096 + k * 1024: gi * 4096 + (k + 1) * 1024]
                    W(e, "t", wt)
                    W(e, "v", w1_)
                    e.scalar_tensor_tensor(ybuf2[su % 2][:, :],
                                           xnsl.bitcast(BF16),
                                           Dsb[:, gi:gi + 1], ypb[:, :],
                                           OP.mult, OP.add).then_inc(SM["v"], 1)
                emit("v", f)
                v_ybuf[su] = CNT["v"]

        def emit_ytr(su):
            k, gi = su // 8, su % 8
            ytr_t = {}
            ytr_a = {}
            for tsub in range(4):
                def f(e, su=su, tsub=tsub, wv=v_ybuf[su],
                      wa=(a_yev.get(su - 1, 0) if tsub < 2 else ytr_a[tsub - 2])):
                    W(e, "v", wv)
                    W(e, "a", wa)
                    e.transpose(ptr[tsub % 2][:, 0:128],
                                ybuf2[su % 2][:, tsub * 128:(tsub + 1) * 128],
                                idn[:, :]).then_inc(SM["t"], 1)
                emit("t", f)
                ytr_t[tsub] = res("t", 1)

                s = Ser("a")
                w1_ = s.op()

                def f(e, su=su, k=k, gi=gi, tsub=tsub, wt=ytr_t[tsub], w1_=w1_,
                      wsi=(si_y.get(k - 1, 0))):
                    W(e, "t", wt)
                    if wsi:
                        W(e, "si", wsi)
                    W(e, "y", YPC * k)
                    W(e, "a", w1_)
                    e.activation(yst[tsub][:, gi * 128:(gi + 1) * 128],
                                 ptr[tsub % 2][:, 0:128],
                                 AF.Identity).then_inc(SM["a"], 1)
                emit("a", f)
                ytr_a[tsub] = CNT["a"]
            t_ytr[su] = ytr_t[3]
            a_yev[su] = ytr_a[3]

            if gi == 7:
                def f(e, k=k, wa=a_yev[su]):
                    W(e, "a", wa)
                    for tsub in range(4):
                        e.dma_start(out=ydr[k * TS + tsub * 128: k * TS + (tsub + 1) * 128, :],
                                    in_=yst[tsub][:, :]).then_inc(SM["y"], 16)
                        if taps:
                            e.dma_start(out=o_ydr[k * TS + tsub * 128: k * TS + (tsub + 1) * 128, :],
                                        in_=yst[tsub][:, :]).then_inc(SM["y"], 16)
                    e.sem_inc(SM["si"], 1)
                emit("sy", f)
                y_ms[k] = res("y", YPC)
                si_y[k] = res("si", 1)

        ccv_ms = {}
        ccv_gc = {}

        def emit_cc(k):
            def f(e, k=k, wy=y_ms[k], wsi=si_y[k]):
                W(e, "si", wsi)
                W(e, "y", wy)
                e.collective_compute(
                    "AllGather", OP.bypass,
                    replica_groups=[[0, 1], [2, 3], [4, 5], [6, 7]],
                    ins=[ydr[k * 512:(k + 1) * 512, :].opt()],
                    outs=[ccd[k * 1024:(k + 1) * 1024, :].opt()],
                ).then_inc(SM["ccv"], 1)
                e.sem_inc(SM["gc"], 1)
            emit("g", f)
            ccv_ms[k] = res("ccv", 1)
            ccv_gc[k] = res("gc", 1)

        for step in range(NHC):
            if step >= 2:
                emit_ytr(2 * step - 4)
                emit_ytr(2 * step - 3)
                if (2 * step - 3) % 8 == 7:
                    emit_cc((2 * step - 3) // 8)
            emit_ffn_strips(step,
                            scan_sus=((2 * step - 2, 2 * step - 1)
                                      if step >= 1 else None))
            emit_ffn_m2(step)
            emit_scan_head(2 * step)
            emit_scan_head(2 * step + 1)
        emit_ytr(2 * NHC - 4)
        emit_ytr(2 * NHC - 3)
        for c in range(4):
            emit_scan_cluster_a(2 * NHC - 2, c)
            emit_scan_cluster(2 * NHC - 2, c)
        for c in range(4):
            emit_scan_cluster_a(2 * NHC - 1, c)
            emit_scan_cluster(2 * NHC - 1, c)
        emit_ytr(2 * NHC - 2)
        emit_ytr(2 * NHC - 1)
        emit_cc(3)

        ACC_TOTAL = CNT["acc"]
        ACC2_TOTAL = CNT["acc2"]
        ccd_flat = ccd.bitcast(U8).rearrange("a b -> (a b)")
        CB = MYCH * 2
        for i in range(2):
            def f(e, i=i, wca=(CNT["cca0"] - 0 if i else 0), wgc=ccv_gc[3]):
                rbC = persist_off(e, "syc4", 2048 * CB, 0)
                base = i * 1024 * CB
                win = ccd_flat[base: base + 2048 * CB + 1024 * CB]
                W(e, "gc", wgc)
                W(e, "ccv", 4)
                if i:
                    W(e, "cca0", wca)
                e.dma_start(
                    out=ccbig[:, :].rearrange("p (s f) -> p s f", s=8),
                    in_=win[ds(rbC, 1024 * CB)]
                        .rearrange("(s p f) -> p s f", s=8, f=CB).bitcast(BF16),
                ).then_inc(SM["ccb"], 16)
                e.sem_inc(SM["si"], 1)
            emit("sy", f)
            ccb_ms = res("ccb", 16)
            si_ccb = res("si", 1)

            for sseg in range(8):
                r, tb = sseg // 4, sseg % 4

                def f(e, i=i, r=r, tb=tb, sseg=sseg, wld=ccb_ms, wsi=si_ccb):
                    W(e, "si", wsi)
                    W(e, "ccb", wld)
                    W(e, "acc", ACC_TOTAL)
                    W(e, "acc2", ACC2_TOTAL)
                    W(e, "xasm", XASM)
                    e.dma_start(
                        out=out[i * 512 + tb * 128: i * 512 + (tb + 1) * 128,
                                r * 1024:(r + 1) * 1024],
                        in_=ccbig[:, sseg * MYCH:(sseg + 1) * MYCH],
                        accum_op=OP.add).then_inc(SM["cca0"], 16)
                emit("g", f)
                res("cca0", 16)

        if taps:
            def f(e, wv=CNT["v"], wa=CNT["a"], wg=GEB):
                W(e, "v", wv)
                W(e, "a", wa)
                W(e, "geB", wg)
                e.dma_start(out=o_xn, in_=xnfm[:, :]).then_inc(SM["tap"], 16)
                e.dma_start(out=o_cols, in_=cols[:, :]).then_inc(SM["tap"], 16)
                e.dma_start(out=o_carry, in_=carry[:, :]).then_inc(SM["tap"], 16)
                e.dma_start(out=o_dtx, in_=dtx[:, :]).then_inc(SM["tap"], 16)
                e.sem_inc(SM["si"], 1)
            emit("sy", f)
            TAPS = res("tap", 64)
            SI_TAP = res("si", 1)

        def f(e, c0=CNT["cca0"]):
            W(e, "acc", ACC_TOTAL)
            W(e, "acc2", ACC2_TOTAL)
            W(e, "cca0", c0)
            W(e, "xasm", XASM)
            if taps:
                W(e, "si", SI_TAP)
                W(e, "tap", TAPS)
        emit("g", f)

        @block.sync
        def _(e):
            for fn in SEQ["sy"]:
                fn(e)

        @block.vector
        def _(e):
            for fn in SEQ["v"]:
                fn(e)

        @block.scalar
        def _(e):
            for fn in SEQ["a"]:
                fn(e)

        @block.tensor
        def _(e):
            for fn in SEQ["t"]:
                fn(e)

        @block.gpsimd
        def _(e):
            for fn in SEQ["g"]:
                fn(e)

    return nc


def make_in_maps(inputs):
    x = np.asarray(inputs["L0_attn_x"], dtype=np.float32)
    W_x = np.asarray(inputs["L0_attn_W_x"], dtype=np.float32)
    W_dt = np.asarray(inputs["L0_attn_W_dt"], dtype=np.float32)
    b_dt = np.asarray(inputs["L0_attn_b_dt"], dtype=np.float32)
    A_log = np.asarray(inputs["L0_attn_A_log"], dtype=np.float32)
    Dv = np.asarray(inputs["L0_attn_D"], dtype=np.float32)
    w_router = np.asarray(inputs["L0_ffn_w_router"], dtype=np.float32)
    w_fc = np.asarray(inputs["L0_ffn_w_fc"], dtype=np.float32)
    w_proj = np.asarray(inputs["L0_ffn_w_proj"], dtype=np.float32)

    wdtb_full = np.concatenate([W_dt.T, b_dt[None, :]], axis=0)
    w2t = np.ascontiguousarray(w_proj.transpose(0, 2, 1))
    ident = np.eye(128, dtype=BF)
    w1t_t = w_fc.transpose(0, 2, 1)
    perm = {0: np.arange(2048),
            1: np.concatenate([np.arange(1024, 2048), np.arange(0, 1024)])}
    wxt_h = {hh: np.ascontiguousarray(W_x.T[perm[hh]]) for hh in (0, 1)}
    wrt_h = {hh: np.ascontiguousarray(w_router[:, perm[hh]]) for hh in (0, 1)}
    w1t_h = {hh: np.ascontiguousarray(w1t_t[:, perm[hh], :]) for hh in (0, 1)}

    in_maps = []
    for c in range(8):
        b, h = c // 2, c % 2
        ch = slice(h * 1024, (h + 1) * 1024)
        in_maps.append({
            "xb": np.ascontiguousarray(x[b]),
            "xbp": np.ascontiguousarray(x[b][:, perm[h]]),
            "wxt": wxt_h[h],
            "wdtb": np.ascontiguousarray(wdtb_full[:, ch]),
            "alog": np.ascontiguousarray(
                A_log[ch].reshape(8, 128, NST).transpose(1, 0, 2).reshape(128, 8 * NST)),
            "dre": np.ascontiguousarray(Dv[ch].reshape(8, 128).T),
            "wrt": wrt_h[h],
            "w1t": w1t_h[h],
            "w2t": w2t,
            "identd": ident,
        })
    return in_maps


def kernel(**inputs):
    nc = build_nc()
    in_maps = make_in_maps(inputs)
    res_ = run_bass_kernel_spmd(nc, in_maps, core_ids=list(range(8)))
    outp = np.empty((4, 2048, 2048), dtype=np.float32)
    for c in range(8):
        b, h = c // 2, c % 2
        outp[b, h * 1024:(h + 1) * 1024, :] = res_.results[c]["out"]
    return outp
